# version 02
# baseline (speedup 1.0000x reference)
import contextlib
import numpy as np
import concourse.bass as bass
import concourse.mybir as mybir
from concourse.bass_utils import run_bass_kernel_spmd

F32 = mybir.dt.float32
BF16 = mybir.dt.bfloat16
AF = mybir.ActivationFunctionType
ALU = mybir.AluOpType
AX = mybir.AxisListType.X

S = 2048
D = 1024
DC = 8
NH = 8
FI = 4096
NE = 8
NU = NE + 1
NTT = 16
NT = 512
TBN = 4
NIG = 8
ICG = 4
EPS_LN = 1e-5
EPS_RT = 1e-9

ENGS = ("tensor", "vector", "scalar", "gpsimd", "sync")
DMA_SEMS = {"ds", "dc", "dg"}
ALL_SEMS = ("pe", "act", "dve", "pool", "ds", "dc", "dg")


class Plan:

    def __init__(self):
        self.ops = {e: [] for e in ENGS}
        self.cnt = {}
        self.waits = []

    def op(self, eng, meth, *args, inc=None, **kwargs):
        step = 16 if inc in DMA_SEMS else 1
        if inc is not None:
            self.cnt[inc] = self.cnt.get(inc, 0) + step
        v = self.cnt.get(inc) if inc is not None else None

        def f(h, sems, meth=meth, args=args, kwargs=kwargs, inc=inc, step=step):
            ins = getattr(h, meth)(*args, **kwargs)
            if inc is not None:
                ins.then_inc(sems[inc], step)
        self.ops[eng].append(f)
        return v

    def wait(self, eng, sem, val):
        if val is None or val <= 0:
            return
        self.waits.append((sem, val))

        def f(h, sems, sem=sem, val=val):
            h.wait_ge(sems[sem], val)
        self.ops[eng].append(f)

    def val(self, sem):
        return self.cnt.get(sem, 0)

    def check(self):
        for sem, val in self.waits:
            assert val <= self.cnt.get(sem, 0), \
                f"wait {sem}>={val} never satisfied (final {self.cnt.get(sem, 0)})"


def build_nc(phases=3, debug=()):
    nc = bass.Bass("TRN2", target_bir_lowering=False)
    debug = set(debug)

    ap = lambda t: t.ap()
    x_d = ap(nc.dram_tensor("x", [S, D], F32, kind="ExternalInput"))
    wq_d = ap(nc.dram_tensor("wq", [NH, D, 128], F32, kind="ExternalInput"))
    wk_d = ap(nc.dram_tensor("wk", [D, 128], F32, kind="ExternalInput"))
    wv_d = ap(nc.dram_tensor("wv", [D, 128], F32, kind="ExternalInput"))
    wot_d = ap(nc.dram_tensor("wot", [NH, 128, D], F32, kind="ExternalInput"))
    gw_d = ap(nc.dram_tensor("gw", [D, NE], F32, kind="ExternalInput"))
    gb_d = ap(nc.dram_tensor("gb", [NE], F32, kind="ExternalInput"))
    w1_d = [ap(nc.dram_tensor("shw1", [D, FI], F32, kind="ExternalInput"))]
    b1_d = [ap(nc.dram_tensor("shb1", [FI], F32, kind="ExternalInput"))]
    w2_d = [ap(nc.dram_tensor("shw2", [FI, D], F32, kind="ExternalInput"))]
    b2_d = [ap(nc.dram_tensor("shb2", [D], F32, kind="ExternalInput"))]
    ew1_d = ap(nc.dram_tensor("ew1", [NE, D, FI], F32, kind="ExternalInput"))
    eb1_d = ap(nc.dram_tensor("eb1", [NE, FI], F32, kind="ExternalInput"))
    ew2_d = ap(nc.dram_tensor("ew2", [NE, FI, D], F32, kind="ExternalInput"))
    eb2_d = ap(nc.dram_tensor("eb2", [NE, D], F32, kind="ExternalInput"))
    for e in range(NE):
        w1_d.append(ew1_d[e])
        b1_d.append(eb1_d[e])
        w2_d.append(ew2_d[e])
        b2_d.append(eb2_d[e])
    out_d = ap(nc.dram_tensor("out", [S, D], F32, kind="ExternalOutput"))
    dbg_d = {}
    if debug & {"resA", "ffn", "g"}:
        dbg_d["res"] = ap(nc.dram_tensor("dbg_res", [128, DC, S], F32, kind="ExternalOutput"))
    if "router" in debug:
        dbg_d["wet"] = ap(nc.dram_tensor("dbg_wet", [NE, S], F32, kind="ExternalOutput"))
    if "ocat" in debug:
        dbg_d["ocat"] = ap(nc.dram_tensor("dbg_ocat", [128, NH * S], BF16, kind="ExternalOutput"))
    if "kv" in debug:
        dbg_d["kv"] = ap(nc.dram_tensor("dbg_kv", [128, 2, S], BF16, kind="ExternalOutput"))

    p = Plan()

    with contextlib.ExitStack() as ctx:
        E = ctx.enter_context
        result = E(nc.sbuf_tensor("result", [128, DC, S], F32))
        xn_fm = E(nc.sbuf_tensor("xn_fm", [128, DC, S], BF16))
        abf = E(nc.sbuf_tensor("abf", [128, 27648], BF16))
        af = E(nc.sbuf_tensor("af", [128, 13312], F32))
        ident_f = E(nc.sbuf_tensor("ident_f", [128, 128], F32))
        ident_b = E(nc.sbuf_tensor("ident_b", [128, 128], BF16))
        ones_col = E(nc.sbuf_tensor("ones_col", [128, 1], BF16))
        ones_row = E(nc.sbuf_tensor("ones_row", [1, 128], F32))
        ebias1 = E(nc.sbuf_tensor("ebias1", [128, NU * 32], F32))
        ebias2 = E(nc.sbuf_tensor("ebias2", [128, NU * 8], F32))
        gatebp = E(nc.sbuf_tensor("gatebp", [128, 1], F32))
        gate_st = E(nc.sbuf_tensor("gate_st", [128, 8, 8], F32))
        rtk = E(nc.sbuf_tensor("rtk", [128, 16], F32))

        bf = abf[:]
        fa = af[:]
        xtokA = fa[:, 0:2048].rearrange("p (b n) -> p b n", b=2)
        xntokA = fa[:, 2048:4096].rearrange("p (b n) -> p b n", b=2)
        sqA = fa[:, 4096:6144].rearrange("p (b n) -> p b n", b=2)
        colsA = fa[:, 6144:6160]
        xnbfA = bf[:, 0:2048].rearrange("p (b n) -> p b n", b=2)
        wstage = fa[:, 0:4096].rearrange("p (b n) -> p b n", b=2)
        wrowstage = fa[0:1, 0:4096]
        wbc = fa[:, 4096:8192].rearrange("p (b n) -> p b n", b=2)
        g1 = fa[:, 8192:9216].rearrange("p (b n) -> p b n", b=2)
        rtok = fa[:, 8192:8320]
        wtok = fa[:, 8320:8448]
        wrow = fa[0:1, 9216:11264]
        wet = fa[0:8, 11264:13312]
        w1bf = bf[:, 0:8192].rearrange("p (b c i) -> p b c i", b=2, c=DC)
        w2bf = bf[:, 8192:16384].rearrange("p (b j d) -> p b j d", b=2, j=ICG)
        h_sb = bf[:, 16384:24576].rearrange("p (j n) -> p j n", j=ICG)
        ocat = bf[:, 0:16384].rearrange("p (h n) -> p h n", h=NH)
        k_sb = bf[:, 16384:18432]
        vt_sb = bf[:, 18432:20480]
        v_sb = bf[:, 20480:22528]
        q_sb = bf[:, 20480:24576].rearrange("p (b n) -> p b n", b=2)
        e_sb = bf[:, 24576:25600].rearrange("p (b n) -> p b n", b=2)
        wqkv_bf = bf[:, 25600:27648].rearrange("p (b c k) -> p b c k", b=2, c=DC)
        wo_bf = bf[:, 16384:24576].rearrange("p (h d) -> p h d", h=NH)
        rb_sb = fa[:, 0:1024].rearrange("p (b n) -> p b n", b=2)
        gt_sb = fa[:, 1024:3072].rearrange("p (b n) -> p b n", b=2)
        xtokC = fa[:, 3072:5120].rearrange("p (b n) -> p b n", b=2)
        wqkv_st = fa[:, 5120:7168].rearrange("p (b c k) -> p b c k", b=2, c=DC)
        wo_st = fa[:, 7168:9216].rearrange("p (b n) -> p b n", b=2)
        sumrec = fa[0:1, 9216:9728]

        psa = E(nc.psum_tensor("psa", [128, 7, 512], F32))
        psx = E(nc.psum_tensor("psx", [128, 1024], BF16))

        sems = {}
        for s_ in ALL_SEMS:
            sems[s_] = E(nc.semaphore(s_))

        guards = {}

        def guard_wait(eng, *keys):
            for key in keys:
                g = guards.get(key)
                if g:
                    p.wait(eng, g[0], g[1])

        def barrier():
            vals = [(s_, p.val(s_)) for s_ in ALL_SEMS]
            for e_ in ENGS:
                for s_, v_ in vals:
                    p.wait(e_, s_, v_)

        p.op("gpsimd", "memset", ident_f[:], 0.0)
        p.op("gpsimd", "memset", ident_b[:], 0.0, inc="pool")
        p.wait("gpsimd", "pool", p.val("pool"))
        p.op("gpsimd", "affine_select", out=ident_f[:], in_=ident_f[:],
             compare_op=ALU.not_equal, fill=1.0, base=0, pattern=[[-1, 128]], channel_multiplier=1)
        p.op("gpsimd", "affine_select", out=ident_b[:], in_=ident_b[:],
             compare_op=ALU.not_equal, fill=1.0, base=0, pattern=[[-1, 128]], channel_multiplier=1)
        p.op("gpsimd", "memset", ones_col[:], 1.0)
        p.op("gpsimd", "memset", ones_row[:], 1.0, inc="pool")
        v_const = p.val("pool")
        p.wait("tensor", "pool", v_const)

        x_rows = x_d.rearrange("(t p) d -> t p d", p=128)
        for tt in range(NTT):
            b = tt % 2
            guard_wait("sync", ("xtokA", b))
            dv = p.op("sync", "dma_start", out=xtokA[:, b, :], in_=x_rows[tt], inc="dc")
            def vfence():
                p.wait("vector", "dve", p.val("dve"))
            p.wait("vector", "dc", dv)
            p.op("vector", "reduce_sum", colsA[:, 0:1], xtokA[:, b, :], axis=AX, inc="dve")
            p.wait("scalar", "dc", dv)
            guard_wait("scalar", ("sqA", b))
            p.op("scalar", "activation", sqA[:, b, :], xtokA[:, b, :], AF.Square, inc="act")
            p.wait("vector", "act", p.val("act"))
            p.op("vector", "reduce_sum", colsA[:, 1:2], sqA[:, b, :], axis=AX, inc="dve")
            vfence()
            p.op("vector", "tensor_scalar_mul", colsA[:, 2:3], colsA[:, 0:1], 1.0 / D)
            p.op("vector", "tensor_scalar_mul", colsA[:, 3:4], colsA[:, 1:2], 1.0 / D, inc="dve")
            vfence()
            p.op("vector", "tensor_mul", colsA[:, 4:5], colsA[:, 2:3], colsA[:, 2:3], inc="dve")
            vfence()
            p.op("vector", "tensor_sub", colsA[:, 5:6], colsA[:, 3:4], colsA[:, 4:5], inc="dve")
            vfence()
            p.op("vector", "tensor_scalar", colsA[:, 5:6], colsA[:, 5:6], EPS_LN, None, ALU.add, inc="dve")
            p.wait("scalar", "dve", p.val("dve"))
            p.op("scalar", "activation", colsA[:, 6:7], colsA[:, 5:6], AF.Sqrt, inc="act")
            p.wait("vector", "act", p.val("act"))
            p.op("vector", "reciprocal", colsA[:, 7:8], colsA[:, 6:7], inc="dve")
            vfence()
            p.op("vector", "scalar_tensor_tensor", colsA[:, 8:9], colsA[:, 2:3], -1.0,
                 colsA[:, 7:8], ALU.mult, ALU.mult, inc="dve")
            vfence()
            guard_wait("vector", ("xntokA", b))
            p.op("vector", "tensor_scalar", xntokA[:, b, :], xtokA[:, b, :],
                 colsA[:, 7:8], colsA[:, 8:9], ALU.mult, ALU.add, inc="dve")
            v_xn = p.val("dve")
            guards[("xtokA", b)] = ("dve", v_xn)
            guards[("sqA", b)] = ("dve", v_xn)
            p.wait("scalar", "dve", v_xn)
            guard_wait("scalar", ("xnbfA", b))
            p.op("scalar", "activation", xnbfA[:, b, :], xntokA[:, b, :], AF.Copy, inc="act")
            v_xnbf = p.val("act")
            p.wait("tensor", "act", v_xnbf)
            guard_wait("tensor", "psx")
            for c in range(DC):
                p.op("tensor", "transpose", psx[:, c * 128:(c + 1) * 128],
                     xnbfA[:, b, c * 128:(c + 1) * 128], ident_b[:],
                     inc="pe" if c == DC - 1 else None)
            v_trb = p.val("pe")
            p.wait("tensor", "dve", v_xn)
            for half in range(2):
                guard_wait("tensor", ("psa", half))
                for c4 in range(4):
                    c = half * 4 + c4
                    p.op("tensor", "transpose", psa[:, half, c4 * 128:(c4 + 1) * 128],
                         xntokA[:, b, c * 128:(c + 1) * 128], ident_f[:],
                         inc="pe" if c4 == 3 else None)
            v_trf = p.val("pe")
            p.wait("vector", "pe", v_trb)
            p.op("vector", "tensor_copy",
                 xn_fm[:, :, tt * 128:(tt + 1) * 128],
                 psx[:].rearrange("p (c n) -> p c n", c=DC), inc="dve")
            guards["psx"] = ("dve", p.val("dve"))
            guards[("xnbfA", b)] = ("pe", v_trb)
            p.wait("vector", "pe", v_trf)
            for half in range(2):
                p.op("vector", "tensor_copy",
                     result[:, half * 4:(half + 1) * 4, tt * 128:(tt + 1) * 128],
                     psa[:, half, :].rearrange("p (c n) -> p c n", c=4), inc="dve")
                guards[("psa", half)] = ("dve", p.val("dve"))
            guards[("xntokA", b)] = ("pe", v_trf)
        vA_dve = p.val("dve")

        barrier()

        if "resA" in debug:
            p.op("gpsimd", "dma_start", out=dbg_d["res"], in_=result[:], inc="dg")

        def bias_row_to_cols(src_row_ap, n128, dst_cols_ap):
            guard_wait("sync", "wrowstage")
            dv = p.op("sync", "dma_start", out=wrowstage[:, 0:n128 * 128], in_=src_row_ap, inc="dc")
            p.wait("tensor", "dc", dv)
            guard_wait("tensor", ("psa", 6))
            for ic in range(n128):
                p.op("tensor", "transpose", psa[0:128, 6, ic:ic + 1],
                     wrowstage[:, ic * 128:(ic + 1) * 128], ident_f[0:1, 0:1],
                     inc="pe" if ic == n128 - 1 else None)
            v = p.val("pe")
            p.wait("vector", "pe", v)
            p.op("vector", "tensor_copy", dst_cols_ap, psa[:, 6, 0:n128], inc="dve")
            guards[("psa", 6)] = ("dve", p.val("dve"))
            guards["wrowstage"] = ("pe", v)

        for u in range(NU):
            bias_row_to_cols(b1_d[u].rearrange("(a i) -> a i", a=1), 32, ebias1[:, u * 32:(u + 1) * 32])
            bias_row_to_cols(b2_d[u].rearrange("(a i) -> a i", a=1), 8, ebias2[:, u * 8:(u + 1) * 8])
        guard_wait("sync", "wrowstage")
        dv = p.op("sync", "dma_start", out=wrowstage[:, 0:8], in_=gb_d.rearrange("(a i) -> a i", a=1), inc="dc")
        p.wait("tensor", "dc", dv)
        guard_wait("tensor", ("psa", 6))
        p.op("tensor", "transpose", psa[0:8, 6, 0:1], wrowstage[:, 0:8], ident_f[0:1, 0:1], inc="pe")
        v = p.val("pe")
        p.wait("vector", "pe", v)
        p.op("vector", "tensor_copy", gatebp[0:8, :], psa[0:8, 6, 0:1], inc="dve")
        guards[("psa", 6)] = ("dve", p.val("dve"))
        p.wait("vector", "dve", p.val("dve"))
        guards["wrowstage"] = ("pe", v)
        dvgw = p.op("sync", "dma_start", out=gate_st[:],
                    in_=gw_d.rearrange("(c p) e -> p c e", p=128), inc="dc")
        gate_sb = wstage[0:8, 0, :]
        p.wait("tensor", "dc", dvgw)
        for tb in range(TBN):
            bank = tb % 2
            guard_wait("tensor", ("psa", bank))
            for c in range(DC):
                p.op("tensor", "matmul", psa[0:8, bank, :], gate_st[:, c, :],
                     result[:, c, tb * NT:(tb + 1) * NT],
                     start=(c == 0), stop=(c == DC - 1),
                     inc="pe" if c == DC - 1 else None)
            v = p.val("pe")
            p.wait("vector", "pe", v)
            guard_wait("vector", "wrowstage")
            p.op("vector", "tensor_scalar", gate_sb[:, tb * NT:(tb + 1) * NT],
                 psa[0:8, bank, :], gatebp[0:8, :], None, ALU.add, inc="dve")
            guards[("psa", bank)] = ("dve", p.val("dve"))
        v_gate = p.val("dve")
        p.wait("tensor", "dve", v_gate)
        for tt in range(NTT):
            bank = 2 + (tt % 2)
            guard_wait("tensor", ("psa", bank))
            p.op("tensor", "transpose", psa[0:128, bank, 0:8],
                 gate_sb[:, tt * 128:(tt + 1) * 128], ident_f[0:8, 0:8], inc="pe")
            v = p.val("pe")
            p.wait("vector", "pe", v)
            p.op("vector", "tensor_copy", rtok[:, tt * 8:(tt + 1) * 8], psa[:, bank, 0:8], inc="dve")
            guards[("psa", bank)] = ("dve", p.val("dve"))
        for tt in range(NTT):
            r8 = rtok[:, tt * 8:(tt + 1) * 8]
            w8 = wtok[:, tt * 8:(tt + 1) * 8]
            c = rtk
            def vf():
                p.wait("vector", "dve", p.val("dve"))
            p.op("vector", "reduce_max", c[:, 0:1], r8, axis=AX, negate=True, inc="dve")
            p.wait("scalar", "dve", p.val("dve"))
            p.op("scalar", "activation", w8, r8, AF.Exp, bias=c[:, 0:1], inc="act")
            p.wait("vector", "act", p.val("act"))
            p.op("vector", "reduce_sum", c[:, 1:2], w8, axis=AX, inc="dve")
            vf()
            p.op("vector", "reciprocal", c[:, 2:3], c[:, 1:2], inc="dve")
            vf()
            p.op("vector", "tensor_scalar", r8, w8, c[:, 2:3], None, ALU.mult, inc="dve")
            vf()
            p.op("vector", "reduce_max", c[:, 3:4], r8, axis=AX, inc="dve")
            vf()
            p.op("vector", "tensor_scalar", c[:, 8:16], r8, c[:, 3:4], None, ALU.is_equal, inc="dve")
            vf()
            p.op("vector", "scalar_tensor_tensor", w8, c[:, 8:16], -2.0, r8, ALU.mult, ALU.add, inc="dve")
            vf()
            p.op("vector", "reduce_max", c[:, 4:5], w8, axis=AX, inc="dve")
            vf()
            p.op("vector", "tensor_scalar", w8, r8, c[:, 4:5], None, ALU.is_ge, inc="dve")
            p.op("vector", "tensor_add", c[:, 5:6], c[:, 3:4], c[:, 4:5], inc="dve")
            vf()
            p.op("vector", "tensor_scalar", c[:, 5:6], c[:, 5:6], EPS_RT, None, ALU.add, inc="dve")
            vf()
            p.op("vector", "reciprocal", c[:, 6:7], c[:, 5:6], inc="dve")
            p.op("vector", "tensor_mul", w8, w8, r8, inc="dve")
            vf()
            p.op("vector", "tensor_scalar", w8, w8, c[:, 6:7], None, ALU.mult, inc="dve")
        v_wtok = p.val("dve")
        p.wait("tensor", "dve", v_wtok)
        for tt in range(NTT):
            bank = 2 + (tt % 2)
            guard_wait("tensor", ("psa", bank))
            p.op("tensor", "transpose", psa[0:8, bank, 0:128],
                 wtok[:, tt * 8:(tt + 1) * 8], ident_f[:], inc="pe")
            v = p.val("pe")
            p.wait("vector", "pe", v)
            p.op("vector", "tensor_copy", wet[:, tt * 128:(tt + 1) * 128],
                 psa[0:8, bank, 0:128], inc="dve")
            guards[("psa", bank)] = ("dve", p.val("dve"))
        v_wet = p.val("dve")
        if "router" in debug:
            p.wait("gpsimd", "dve", v_wet)
            p.op("gpsimd", "dma_start", out=dbg_d["wet"], in_=wet, inc="dg")

        if phases >= 2:
            barrier()

            wbc_pool_guard = {}

            def build_wbc(e):
                guard_wait("gpsimd", "wrow")
                dv = p.op("gpsimd", "dma_start", out=wrow, in_=wet[e:e + 1, :], inc="dg")
                p.wait("tensor", "dg", dv)
                pg = wbc_pool_guard.get(e % 2)
                if pg:
                    p.wait("vector", "pool", pg)
                for tb in range(TBN):
                    guard_wait("tensor", ("psa", 6))
                    p.op("tensor", "matmul", psa[:, 6, :], ones_row[:],
                         wrow[:, tb * NT:(tb + 1) * NT], start=True, stop=True, inc="pe")
                    v = p.val("pe")
                    guards["wrow"] = ("pe", v)
                    p.wait("vector", "pe", v)
                    guard_wait("vector", ("wbc", e % 2))
                    p.op("vector", "tensor_copy", wbc[:, e % 2, tb * NT:(tb + 1) * NT],
                         psa[:, 6, :], inc="dve")
                    guards[("psa", 6)] = ("dve", p.val("dve"))
                return p.val("dve")

            wstage_slot = [0]

            def dma_w1(u, ig):
                dvs = []
                for half in range(2):
                    slot = wstage_slot[0]; wstage_slot[0] ^= 1
                    guard_wait("sync", ("wstage", slot))
                    src = w1_d[u].rearrange("(c p) i -> p c i", p=128)[
                        :, :, ig * 512 + half * 256: ig * 512 + (half + 1) * 256]
                    dv = p.op("sync", "dma_start",
                              out=wstage[:, slot, :].rearrange("p (c i) -> p c i", c=DC),
                              in_=src, inc="ds")
                    dvs.append((slot, dv))
                return dvs

            def conv_w1(ig, dvs):
                buf = ig % 2
                guard_wait("scalar", ("w1bf", buf))
                for half, (slot, dv) in enumerate(dvs):
                    p.wait("scalar", "ds", dv)
                    p.op("scalar", "activation",
                         w1bf[:, buf, :, half * 256:(half + 1) * 256],
                         wstage[:, slot, :].rearrange("p (c i) -> p c i", c=DC), AF.Copy, inc="act")
                    guards[("wstage", slot)] = ("act", p.val("act"))
                return p.val("act")

            def dma_w2(u, ig):
                dvs = []
                for half in range(2):
                    slot = wstage_slot[0]; wstage_slot[0] ^= 1
                    guard_wait("sync", ("wstage", slot))
                    src = w2_d[u].rearrange("(g p) d -> p g d", p=128)[
                        :, ig * ICG + half * 2: ig * ICG + (half + 1) * 2, :]
                    dv = p.op("sync", "dma_start",
                              out=wstage[:, slot, :].rearrange("p (g d) -> p g d", g=2),
                              in_=src, inc="ds")
                    dvs.append((slot, dv))
                return dvs

            def conv_w2(ig, dvs):
                buf = ig % 2
                guard_wait("scalar", ("w2bf", buf))
                for half, (slot, dv) in enumerate(dvs):
                    p.wait("scalar", "ds", dv)
                    p.op("scalar", "activation",
                         w2bf[:, buf, half * 2:(half + 1) * 2, :],
                         wstage[:, slot, :].rearrange("p (j d) -> p j d", j=2), AF.Copy, inc="act")
                    guards[("wstage", slot)] = ("act", p.val("act"))
                return p.val("act")

            wbc_ready = {0: build_wbc(0)}
            d1 = dma_w1(0, 0)
            w1_ready = conv_w1(0, d1)
            d2 = dma_w2(0, 0)

            gelu_j = [0]
            units = [(u, ig) for u in range(NU) for ig in range(NIG)]
            for idx, (u, ig) in enumerate(units):
                is_exp = u >= 1
                e = u - 1
                buf = ig % 2
                if is_exp and ig == 0:
                    p.wait("gpsimd", "dve", wbc_ready[e])
                w2_ready = conv_w2(ig, d2)
                nxt = units[idx + 1] if idx + 1 < len(units) else None
                if nxt:
                    d1 = dma_w1(*nxt)
                p.wait("tensor", "act", w1_ready)
                h_done = {}
                for ic in range(ICG):
                    icg = ig * ICG + ic
                    for tb in range(TBN):
                        bank = (ic * TBN + tb) % 3
                        guard_wait("tensor", ("psa", bank))
                        for c in range(DC):
                            p.op("tensor", "matmul", psa[:, bank, :],
                                 w1bf[:, buf, c, ic * 128:(ic + 1) * 128],
                                 xn_fm[:, c, tb * NT:(tb + 1) * NT],
                                 start=(c == 0), stop=(c == DC - 1),
                                 inc="pe" if c == DC - 1 else None)
                        v_h = p.val("pe")
                        guards[("w1bf", buf)] = ("pe", v_h)
                        p.wait("scalar", "pe", v_h)
                        bias_ap = ebias1[:, u * 32 + icg:u * 32 + icg + 1]
                        if is_exp:
                            j = gelu_j[0] % 2
                            gelu_j[0] += 1
                            guard_wait("scalar", ("g1", j))
                            p.op("scalar", "activation", g1[:, j, :], psa[:, bank, :],
                                 AF.Gelu_apprx_tanh, bias=bias_ap, inc="act")
                            v_gl = p.val("act")
                            guards[("psa", bank)] = ("act", v_gl)
                            p.wait("gpsimd", "act", v_gl)
                            guard_wait("gpsimd", "h_sb")
                            p.op("gpsimd", "tensor_mul", h_sb[:, ic, tb * NT:(tb + 1) * NT],
                                 g1[:, j, :], wbc[:, e % 2, tb * NT:(tb + 1) * NT], inc="pool")
                            guards[("g1", j)] = ("pool", p.val("pool"))
                            h_done[(ic, tb)] = ("pool", p.val("pool"))
                        else:
                            guard_wait("scalar", "h_sb")
                            p.op("scalar", "activation", h_sb[:, ic, tb * NT:(tb + 1) * NT],
                                 psa[:, bank, :], AF.Gelu_apprx_tanh, bias=bias_ap, inc="act")
                            guards[("psa", bank)] = ("act", p.val("act"))
                            h_done[(ic, tb)] = ("act", p.val("act"))
                if is_exp:
                    wbc_pool_guard[e % 2] = p.val("pool")
                if nxt:
                    w1_ready = conv_w1(nxt[1], d1)
                    d2 = dma_w2(*nxt)
                if ig == 0 and u + 1 < NU:
                    wbc_ready[u] = build_wbc(u)
                p.wait("tensor", "act", w2_ready)
                for dcc in range(DC):
                    for tb in range(TBN):
                        bank = 3 + (dcc * TBN + tb) % 3
                        guard_wait("tensor", ("psa", bank))
                        sem_h, v_hh = h_done[(ICG - 1, tb)]
                        p.wait("tensor", sem_h, v_hh)
                        for j in range(ICG):
                            p.op("tensor", "matmul", psa[:, bank, :],
                                 w2bf[:, buf, j, dcc * 128:(dcc + 1) * 128],
                                 h_sb[:, j, tb * NT:(tb + 1) * NT],
                                 start=(j == 0), stop=(j == ICG - 1),
                                 inc="pe" if j == ICG - 1 else None)
                        v_o = p.val("pe")
                        guards[("w2bf", buf)] = ("pe", v_o)
                        p.wait("vector", "pe", v_o)
                        p.op("vector", "tensor_tensor",
                             result[:, dcc, tb * NT:(tb + 1) * NT],
                             result[:, dcc, tb * NT:(tb + 1) * NT],
                             psa[:, bank, :], ALU.add, inc="dve")
                        guards[("psa", bank)] = ("dve", p.val("dve"))
                guards["h_sb"] = ("pe", p.val("pe"))
                if ig == NIG - 1:
                    if is_exp:
                        for dcc in range(DC):
                            p.op("vector", "scalar_tensor_tensor",
                                 result[:, dcc, :], wbc[:, e % 2, :],
                                 ebias2[:, u * 8 + dcc:u * 8 + dcc + 1],
                                 result[:, dcc, :], ALU.mult, ALU.add, inc="dve")
                        guards[("wbc", e % 2)] = ("dve", p.val("dve"))
                    else:
                        for dcc in range(DC):
                            p.op("vector", "tensor_scalar", result[:, dcc, :], result[:, dcc, :],
                                 ebias2[:, u * 8 + dcc:u * 8 + dcc + 1], None, ALU.add, inc="dve")
            v_ffn = p.val("dve")
            if "ffn" in debug:
                p.wait("gpsimd", "dve", v_ffn)
                p.op("gpsimd", "dma_start", out=dbg_d["res"], in_=result[:], inc="dg")

        if phases >= 3:
            barrier()

            for dcc in range(DC):
                p.op("scalar", "activation", result[:, dcc, :], result[:, dcc, :],
                     AF.Silu, inc="act")
            v_g = p.val("act")
            if "g" in debug:
                p.wait("gpsimd", "act", v_g)
                p.op("gpsimd", "dma_start", out=dbg_d["res"], in_=result[:], inc="dg")

            def stream_qkv(src_ap, slot):
                guard_wait("sync", ("wqkv_st", slot))
                dv = p.op("sync", "dma_start", out=wqkv_st[:, slot, :, :],
                          in_=src_ap.rearrange("(c p) k -> p c k", p=128), inc="ds")
                p.wait("scalar", "ds", dv)
                guard_wait("scalar", ("wqkv_bf", slot))
                p.op("scalar", "activation",
                     wqkv_bf[:, slot, :, :].rearrange("p c k -> p (c k)"),
                     wqkv_st[:, slot, :, :].rearrange("p c k -> p (c k)"), AF.Copy, inc="act")
                guards[("wqkv_st", slot)] = ("act", p.val("act"))
                return p.val("act")

            def proj_2048(slot, dst, vready, banks=(0, 1)):
                p.wait("tensor", "act", vready)
                for nb in range(TBN):
                    bank = banks[nb % len(banks)]
                    guard_wait("tensor", ("psa", bank))
                    for c in range(DC):
                        p.op("tensor", "matmul", psa[:, bank, :],
                             wqkv_bf[:, slot, c, :],
                             xn_fm[:, c, nb * NT:(nb + 1) * NT],
                             start=(c == 0), stop=(c == DC - 1),
                             inc="pe" if c == DC - 1 else None)
                    v = p.val("pe")
                    guards[("wqkv_bf", slot)] = ("pe", v)
                    p.wait("vector", "pe", v)
                    p.op("vector", "tensor_copy", dst[:, nb * NT:(nb + 1) * NT],
                         psa[:, bank, :], inc="dve")
                    guards[("psa", bank)] = ("dve", p.val("dve"))
                return p.val("dve")

            v_ksb = proj_2048(0, k_sb, stream_qkv(wk_d, 0))
            v_vsb = proj_2048(1, v_sb, stream_qkv(wv_d, 1))
            p.wait("tensor", "dve", v_vsb)
            for rnd in range(2):
                guard_wait("tensor", "psx")
                for t8 in range(8):
                    mc = rnd * 8 + t8
                    p.op("tensor", "transpose", psx[:, t8 * 128:(t8 + 1) * 128],
                         v_sb[:, mc * 128:(mc + 1) * 128], ident_b[:],
                         inc="pe" if t8 == 7 else None)
                v = p.val("pe")
                p.wait("vector", "pe", v)
                p.op("vector", "tensor_copy", vt_sb[:, rnd * 1024:(rnd + 1) * 1024],
                     psx[:], inc="dve")
                guards["psx"] = ("dve", p.val("dve"))
            v_vt_pe = p.val("pe")
            v_vt = p.val("dve")
            if "kv" in debug:
                p.wait("gpsimd", "dve", v_vt)
                p.op("gpsimd", "dma_start", out=dbg_d["kv"][:, 0, :], in_=k_sb, inc="dg")
                p.op("gpsimd", "dma_start", out=dbg_d["kv"][:, 1, :], in_=vt_sb, inc="dg")

            def proj_q(h):
                slot = h % 2
                vr = stream_qkv(wq_d[h], slot)
                p.wait("tensor", "act", vr)
                for nb in range(TBN):
                    guard_wait("tensor", ("psa", 5))
                    for c in range(DC):
                        p.op("tensor", "matmul", psa[:, 5, :],
                             wqkv_bf[:, slot, c, :],
                             xn_fm[:, c, nb * NT:(nb + 1) * NT],
                             start=(c == 0), stop=(c == DC - 1),
                             inc="pe" if c == DC - 1 else None)
                    v = p.val("pe")
                    guards[("wqkv_bf", slot)] = ("pe", v)
                    p.wait("vector", "pe", v)
                    guard_wait("vector", ("q_sb", h % 2))
                    p.op("vector", "tensor_copy", q_sb[:, h % 2, nb * NT:(nb + 1) * NT],
                         psa[:, 5, :], inc="dve")
                    guards[("psa", 5)] = ("dve", p.val("dve"))
                return p.val("dve")

            guards[("q_sb", 0)] = ("pe", v_vt_pe)
            q_ready = proj_q(0)

            for h in range(NH):
                p.wait("tensor", "dve", q_ready)
                for nb in range(TBN):
                    otbank = 2 + nb % 2
                    guard_wait("tensor", ("psa", otbank), ("psa", 4))
                    for mc in range(16):
                        ltbank = mc % 2
                        guard_wait("tensor", ("psa", ltbank))
                        p.op("tensor", "matmul", psa[:, ltbank, :],
                             k_sb[:, mc * 128:(mc + 1) * 128],
                             q_sb[:, h % 2, nb * NT:(nb + 1) * NT],
                             start=True, stop=True, inc="pe")
                        v_l = p.val("pe")
                        p.wait("scalar", "pe", v_l)
                        guard_wait("scalar", ("e_sb", mc % 2))
                        p.op("scalar", "activation", e_sb[:, mc % 2, :], psa[:, ltbank, :],
                             AF.Exp, inc="act")
                        v_e = p.val("act")
                        guards[("psa", ltbank)] = ("act", v_e)
                        p.wait("tensor", "act", v_e)
                        p.op("tensor", "matmul", psa[:, otbank, :],
                             vt_sb[:, mc * 128:(mc + 1) * 128], e_sb[:, mc % 2, :],
                             start=(mc == 0), stop=(mc == 15))
                        p.op("tensor", "matmul", psa[0:1, 4, :],
                             ones_col[:], e_sb[:, mc % 2, :],
                             start=(mc == 0), stop=(mc == 15), inc="pe")
                        guards[("e_sb", mc % 2)] = ("pe", p.val("pe"))
                    v_osum = p.val("pe")
                    p.wait("vector", "pe", v_osum)
                    guard_wait("vector", "sumrec")
                    p.op("vector", "reciprocal", sumrec, psa[0:1, 4, :], inc="dve")
                    v_sr = p.val("dve")
                    guards[("psa", 4)] = ("dve", v_sr)
                    p.wait("tensor", "dve", v_sr)
                    guard_wait("tensor", ("psa", 6))
                    p.op("tensor", "matmul", psa[:, 6, :], ones_row[:], sumrec,
                         start=True, stop=True, inc="pe")
                    v_rb = p.val("pe")
                    guards["sumrec"] = ("pe", v_rb)
                    p.wait("vector", "pe", v_rb)
                    p.op("vector", "tensor_copy", rb_sb[:, nb % 2, :], psa[:, 6, :], inc="dve")
                    guards[("psa", 6)] = ("dve", p.val("dve"))
                    p.wait("vector", "dve", p.val("dve"))
                    p.op("vector", "tensor_tensor", ocat[:, h, nb * NT:(nb + 1) * NT],
                         psa[:, otbank, :], rb_sb[:, nb % 2, :], ALU.mult, inc="dve")
                    guards[("psa", otbank)] = ("dve", p.val("dve"))
                guards[("q_sb", h % 2)] = ("pe", p.val("pe"))
                if h + 1 < NH:
                    q_ready = proj_q(h + 1)
            v_attn_pe = p.val("pe")
            v_ocat = p.val("dve")
            if "ocat" in debug:
                p.wait("gpsimd", "dve", v_ocat)
                p.op("gpsimd", "dma_start", out=dbg_d["ocat"],
                     in_=ocat.rearrange("p h n -> p (h n)"), inc="dg")

            wo_slot = [0]
            for hh in range(NH):
                slot = wo_slot[0]; wo_slot[0] ^= 1
                guard_wait("sync", ("wo_st", slot))
                dv = p.op("sync", "dma_start", out=wo_st[:, slot, :], in_=wot_d[hh], inc="ds")
                p.wait("scalar", "ds", dv)
                if hh == 0:
                    p.wait("scalar", "pe", v_attn_pe)
                p.op("scalar", "activation", wo_bf[:, hh, :], wo_st[:, slot, :], AF.Copy, inc="act")
                guards[("wo_st", slot)] = ("act", p.val("act"))
            v_wo = p.val("act")

            out_rows = out_d.rearrange("(t p) d -> t p d", p=128)
            xtok_dma = {}
            for ns in range(2):
                xtok_dma[ns] = p.op("sync", "dma_start", out=xtokC[:, ns % 2, :],
                                    in_=x_rows[ns], inc="ds")
            p.wait("tensor", "act", v_wo)
            p.wait("tensor", "dve", v_ocat)
            for ns in range(NTT):
                yb = 2 * (ns % 2)
                gbank = 4 + ns % 2
                guard_wait("tensor", ("psa", yb), ("psa", yb + 1))
                for hh in range(NH):
                    for half in range(2):
                        p.op("tensor", "matmul", psa[:, yb + half, :],
                             ocat[:, hh, ns * 128:(ns + 1) * 128],
                             wo_bf[:, hh, half * 512:(half + 1) * 512],
                             start=(hh == 0), stop=(hh == NH - 1),
                             inc="pe" if (hh == NH - 1 and half == 1) else None)
                v_y = p.val("pe")
                for half in range(2):
                    guard_wait("tensor", ("psa", gbank))
                    for c4 in range(4):
                        c = half * 4 + c4
                        p.op("tensor", "transpose", psa[:, gbank, c4 * 128:(c4 + 1) * 128],
                             result[:, c, ns * 128:(ns + 1) * 128], ident_f[:],
                             inc="pe" if c4 == 3 else None)
                    v_t = p.val("pe")
                    p.wait("vector", "pe", v_t)
                    guard_wait("vector", ("gt", ns % 2))
                    p.op("vector", "tensor_copy",
                         gt_sb[:, ns % 2, half * 512:(half + 1) * 512],
                         psa[:, gbank, :], inc="dve")
                    guards[("psa", gbank)] = ("dve", p.val("dve"))
                p.wait("vector", "pe", v_y)
                p.wait("vector", "ds", xtok_dma[ns])
                p.wait("vector", "dve", p.val("dve"))
                yview = psa[:, yb:yb + 2, :].rearrange("p a b -> p (a b)")
                p.op("vector", "tensor_tensor", gt_sb[:, ns % 2, :], yview,
                     gt_sb[:, ns % 2, :], ALU.mult, inc="dve")
                p.wait("vector", "dve", p.val("dve"))
                p.op("vector", "tensor_tensor", xtokC[:, ns % 2, :], xtokC[:, ns % 2, :],
                     gt_sb[:, ns % 2, :], ALU.add, inc="dve")
                v_ep = p.val("dve")
                guards[("psa", yb)] = ("dve", v_ep)
                guards[("psa", yb + 1)] = ("dve", v_ep)
                guards[("gt", ns % 2)] = ("dve", v_ep)
                p.wait("sync", "dve", v_ep)
                p.op("sync", "dma_start", out=out_rows[ns], in_=xtokC[:, ns % 2, :], inc="dc")
                guards[("xtokC", ns % 2)] = ("dc", p.val("dc"))
                if ns + 2 < NTT:
                    guard_wait("sync", ("xtokC", ns % 2))
                    xtok_dma[ns + 2] = p.op("sync", "dma_start", out=xtokC[:, ns % 2, :],
                                            in_=x_rows[ns + 2], inc="ds")
            p.wait("scalar", "dc", p.val("dc"))
            if debug:
                p.wait("gpsimd", "dg", p.val("dg"))
        else:
            if debug:
                p.wait("gpsimd", "dg", p.val("dg"))
            p.wait("gpsimd", "dve", p.val("dve"))
            p.op("gpsimd", "dma_start",
                 out=out_d.rearrange("(c p) d -> p c d", p=128),
                 in_=result[:].rearrange("p c n -> p (c n)").rearrange("p (a b) -> p a b", a=16),
                 inc="dg")
            p.wait("gpsimd", "dg", p.val("dg"))

        p.check()

        with nc.Block() as block:
            @block.tensor
            def _(h):
                for f in p.ops["tensor"]:
                    f(h, sems)

            @block.vector
            def _(h):
                for f in p.ops["vector"]:
                    f(h, sems)

            @block.scalar
            def _(h):
                for f in p.ops["scalar"]:
                    f(h, sems)

            @block.gpsimd
            def _(h):
                for f in p.ops["gpsimd"]:
                    f(h, sems)

            @block.sync
            def _(h):
                for f in p.ops["sync"]:
                    f(h, sems)

    return nc


def make_in_maps(L0_attn_x, L0_attn_proj_q, L0_attn_proj_k, L0_attn_proj_v, L0_attn_proj_o,
                 L0_ffn_gate_w, L0_ffn_gate_b, L0_ffn_sh_w1, L0_ffn_sh_b1, L0_ffn_sh_w2,
                 L0_ffn_sh_b2, L0_ffn_ew1, L0_ffn_eb1, L0_ffn_ew2, L0_ffn_eb2):
    f32 = np.float32
    common = {
        "wq": np.ascontiguousarray(L0_attn_proj_q, f32),
        "wk": np.ascontiguousarray(L0_attn_proj_k, f32),
        "wv": np.ascontiguousarray(L0_attn_proj_v, f32),
        "wot": np.ascontiguousarray(np.asarray(L0_attn_proj_o, f32).transpose(0, 2, 1)),
        "gw": np.ascontiguousarray(L0_ffn_gate_w, f32),
        "gb": np.ascontiguousarray(L0_ffn_gate_b, f32),
        "shw1": np.ascontiguousarray(L0_ffn_sh_w1, f32),
        "shb1": np.ascontiguousarray(L0_ffn_sh_b1, f32),
        "shw2": np.ascontiguousarray(L0_ffn_sh_w2, f32),
        "shb2": np.ascontiguousarray(L0_ffn_sh_b2, f32),
        "ew1": np.ascontiguousarray(L0_ffn_ew1, f32),
        "eb1": np.ascontiguousarray(L0_ffn_eb1, f32),
        "ew2": np.ascontiguousarray(L0_ffn_ew2, f32),
        "eb2": np.ascontiguousarray(L0_ffn_eb2, f32),
    }
    xs = np.ascontiguousarray(L0_attn_x, f32)
    return [dict(common, x=xs[c]) for c in range(8)]


def kernel(**inputs):
    nc = build_nc()
    in_maps = make_in_maps(**inputs)
    res = run_bass_kernel_spmd(nc, in_maps, core_ids=list(range(8)))
    return np.stack([res.results[c]["out"] for c in range(8)], axis=0)


# version 03
# speedup vs baseline: 1.1134x; 1.1134x over previous
import contextlib
import numpy as np
import concourse.bass as bass
import concourse.mybir as mybir
from concourse.bass_utils import run_bass_kernel_spmd

F32 = mybir.dt.float32
BF16 = mybir.dt.bfloat16
AF = mybir.ActivationFunctionType
ALU = mybir.AluOpType
AX = mybir.AxisListType.X

S = 2048
D = 1024
DC = 8
NH = 8
FI = 4096
NE = 8
NU = NE + 1
NTT = 16
NT = 512
TBN = 4
NIG = 8
ICG = 4
EPS_LN = 1e-5
EPS_RT = 1e-9

ENGS = ("tensor", "vector", "scalar", "gpsimd", "sync")
DMA_SEMS = {"ds", "dc", "dg"}
ALL_SEMS = ("pe", "act", "dve", "pool", "ds", "dc", "dg")


class Plan:

    def __init__(self):
        self.ops = {e: [] for e in ENGS}
        self.cnt = {}
        self.waits = []

    def op(self, eng, meth, *args, inc=None, **kwargs):
        step = 16 if inc in DMA_SEMS else 1
        if inc is not None:
            self.cnt[inc] = self.cnt.get(inc, 0) + step
        v = self.cnt.get(inc) if inc is not None else None

        def f(h, sems, meth=meth, args=args, kwargs=kwargs, inc=inc, step=step):
            ins = getattr(h, meth)(*args, **kwargs)
            if inc is not None:
                ins.then_inc(sems[inc], step)
        self.ops[eng].append(f)
        return v

    def wait(self, eng, sem, val):
        if val is None or val <= 0:
            return
        self.waits.append((sem, val))

        def f(h, sems, sem=sem, val=val):
            h.wait_ge(sems[sem], val)
        self.ops[eng].append(f)

    def val(self, sem):
        return self.cnt.get(sem, 0)

    def check(self):
        for sem, val in self.waits:
            assert val <= self.cnt.get(sem, 0), \
                f"wait {sem}>={val} never satisfied (final {self.cnt.get(sem, 0)})"


def build_nc(phases=3, debug=()):
    nc = bass.Bass("TRN2", target_bir_lowering=False)
    debug = set(debug)

    ap = lambda t: t.ap()
    x_d = ap(nc.dram_tensor("x", [S, D], F32, kind="ExternalInput"))
    wq_d = ap(nc.dram_tensor("wq", [NH, D, 128], F32, kind="ExternalInput"))
    wk_d = ap(nc.dram_tensor("wk", [D, 128], F32, kind="ExternalInput"))
    wv_d = ap(nc.dram_tensor("wv", [D, 128], F32, kind="ExternalInput"))
    wot_d = ap(nc.dram_tensor("wot", [NH, 128, D], F32, kind="ExternalInput"))
    gw_d = ap(nc.dram_tensor("gw", [D, NE], F32, kind="ExternalInput"))
    gb_d = ap(nc.dram_tensor("gb", [NE], F32, kind="ExternalInput"))
    w1_d = [ap(nc.dram_tensor("shw1", [D, FI], F32, kind="ExternalInput"))]
    b1_d = [ap(nc.dram_tensor("shb1", [FI], F32, kind="ExternalInput"))]
    w2_d = [ap(nc.dram_tensor("shw2", [FI, D], F32, kind="ExternalInput"))]
    b2_d = [ap(nc.dram_tensor("shb2", [D], F32, kind="ExternalInput"))]
    ew1_d = ap(nc.dram_tensor("ew1", [NE, D, FI], F32, kind="ExternalInput"))
    eb1_d = ap(nc.dram_tensor("eb1", [NE, FI], F32, kind="ExternalInput"))
    ew2_d = ap(nc.dram_tensor("ew2", [NE, FI, D], F32, kind="ExternalInput"))
    eb2_d = ap(nc.dram_tensor("eb2", [NE, D], F32, kind="ExternalInput"))
    for e in range(NE):
        w1_d.append(ew1_d[e])
        b1_d.append(eb1_d[e])
        w2_d.append(ew2_d[e])
        b2_d.append(eb2_d[e])
    out_d = ap(nc.dram_tensor("out", [S, D], F32, kind="ExternalOutput"))
    dbg_d = {}
    if debug & {"resA", "ffn", "g"}:
        dbg_d["res"] = ap(nc.dram_tensor("dbg_res", [128, DC, S], F32, kind="ExternalOutput"))
    if "router" in debug:
        dbg_d["wet"] = ap(nc.dram_tensor("dbg_wet", [NE, S], F32, kind="ExternalOutput"))
    if "router2" in debug:
        dbg_d["gbp"] = ap(nc.dram_tensor("dbg_gbp", [8, 1], F32, kind="ExternalOutput"))
        dbg_d["grow"] = ap(nc.dram_tensor("dbg_grow", [1, 8], F32, kind="ExternalOutput"))
        dbg_d["gsb"] = ap(nc.dram_tensor("dbg_gsb", [8, S], F32, kind="ExternalOutput"))
        dbg_d["rtok"] = ap(nc.dram_tensor("dbg_rtok", [128, 128], F32, kind="ExternalOutput"))
        dbg_d["wtok"] = ap(nc.dram_tensor("dbg_wtok", [128, 128], F32, kind="ExternalOutput"))
    if "ocat" in debug:
        dbg_d["ocat"] = ap(nc.dram_tensor("dbg_ocat", [128, NH * S], BF16, kind="ExternalOutput"))
    if "kv" in debug:
        dbg_d["kv"] = ap(nc.dram_tensor("dbg_kv", [128, 2, S], BF16, kind="ExternalOutput"))

    p = Plan()

    with contextlib.ExitStack() as ctx:
        E = ctx.enter_context
        result = E(nc.sbuf_tensor("result", [128, DC, S], F32))
        xn_fm = E(nc.sbuf_tensor("xn_fm", [128, DC, S], BF16))
        abf = E(nc.sbuf_tensor("abf", [128, 27648], BF16))
        af = E(nc.sbuf_tensor("af", [128, 13312], F32))
        ident_f = E(nc.sbuf_tensor("ident_f", [128, 128], F32))
        ident_b = E(nc.sbuf_tensor("ident_b", [128, 128], BF16))
        ones_col = E(nc.sbuf_tensor("ones_col", [128, 1], BF16))
        ones_row = E(nc.sbuf_tensor("ones_row", [1, 128], F32))
        ebias1 = E(nc.sbuf_tensor("ebias1", [128, NU * 32], F32))
        ebias2 = E(nc.sbuf_tensor("ebias2", [128, NU * 8], F32))
        gatebp = E(nc.sbuf_tensor("gatebp", [128, 1], F32))
        gate_st = E(nc.sbuf_tensor("gate_st", [128, 8, 8], F32))
        rtk = E(nc.sbuf_tensor("rtk", [128, 16], F32))

        bf = abf[:]
        fa = af[:]
        xtokA = fa[:, 0:2048].rearrange("p (b n) -> p b n", b=2)
        xntokA = fa[:, 2048:4096].rearrange("p (b n) -> p b n", b=2)
        sqA = fa[:, 4096:6144].rearrange("p (b n) -> p b n", b=2)
        colsA = fa[:, 6144:6160]
        xnbfA = bf[:, 0:2048].rearrange("p (b n) -> p b n", b=2)
        wstage = fa[:, 0:4096].rearrange("p (b n) -> p b n", b=2)
        wrowstage = fa[0:1, 0:4096]
        wbc = fa[:, 4096:8192].rearrange("p (b n) -> p b n", b=2)
        g1 = fa[:, 8192:9216].rearrange("p (b n) -> p b n", b=2)
        rtok = fa[:, 8192:8320]
        wtok = fa[:, 8320:8448]
        wrow = fa[0:1, 9216:11264]
        wet = fa[0:8, 11264:13312]
        w1bf = bf[:, 0:8192].rearrange("p (b c i) -> p b c i", b=2, c=DC)
        w2bf = bf[:, 8192:16384].rearrange("p (b j d) -> p b j d", b=2, j=ICG)
        h_sb = bf[:, 16384:24576].rearrange("p (j n) -> p j n", j=ICG)
        ocat = bf[:, 0:16384].rearrange("p (h n) -> p h n", h=NH)
        k_sb = bf[:, 16384:18432]
        vt_sb = bf[:, 18432:20480]
        v_sb = bf[:, 20480:22528]
        q_sb = bf[:, 20480:24576].rearrange("p (b n) -> p b n", b=2)
        e_sb = bf[:, 24576:25600].rearrange("p (b n) -> p b n", b=2)
        wqkv_bf = bf[:, 25600:27648].rearrange("p (b c k) -> p b c k", b=2, c=DC)
        wo_bf = bf[:, 16384:24576].rearrange("p (h d) -> p h d", h=NH)
        rb_sb = fa[:, 0:1024].rearrange("p (b n) -> p b n", b=2)
        gt_sb = fa[:, 1024:3072].rearrange("p (b n) -> p b n", b=2)
        xtokC = fa[:, 3072:5120].rearrange("p (b n) -> p b n", b=2)
        wqkv_st = fa[:, 5120:7168].rearrange("p (b c k) -> p b c k", b=2, c=DC)
        wo_st = fa[:, 7168:9216].rearrange("p (b n) -> p b n", b=2)
        sumrec = fa[0:1, 9216:9728]

        psa = E(nc.psum_tensor("psa", [128, 7, 512], F32))
        psx = E(nc.psum_tensor("psx", [128, 1024], BF16))

        sems = {}
        for s_ in ALL_SEMS:
            sems[s_] = E(nc.semaphore(s_))

        guards = {}

        def guard_wait(eng, *keys):
            for key in keys:
                g = guards.get(key)
                if g:
                    p.wait(eng, g[0], g[1])

        def barrier():
            vals = [(s_, p.val(s_)) for s_ in ALL_SEMS]
            for e_ in ENGS:
                for s_, v_ in vals:
                    p.wait(e_, s_, v_)

        def pe_flush():
            p.op("tensor", "matmul", psa[0:128, 6, 256:384], ident_b[:],
                 ident_b[:], start=True, stop=True)
            p.op("tensor", "matmul", psa[0:128, 6, 384:512], ident_b[:],
                 ident_b[:], start=True, stop=True, inc="pe")
            return p.val("pe")

        p.op("gpsimd", "memset", ident_f[:], 0.0)
        p.op("gpsimd", "memset", ident_b[:], 0.0, inc="pool")
        p.wait("gpsimd", "pool", p.val("pool"))
        p.op("gpsimd", "affine_select", out=ident_f[:], in_=ident_f[:],
             compare_op=ALU.not_equal, fill=1.0, base=0, pattern=[[-1, 128]], channel_multiplier=1)
        p.op("gpsimd", "affine_select", out=ident_b[:], in_=ident_b[:],
             compare_op=ALU.not_equal, fill=1.0, base=0, pattern=[[-1, 128]], channel_multiplier=1)
        p.op("gpsimd", "memset", ones_col[:], 1.0)
        p.op("gpsimd", "memset", ones_row[:], 1.0, inc="pool")
        v_const = p.val("pool")
        p.wait("tensor", "pool", v_const)

        x_rows = x_d.rearrange("(t p) d -> t p d", p=128)
        for tt in range(NTT):
            b = tt % 2
            guard_wait("sync", ("xtokA", b))
            dv = p.op("sync", "dma_start", out=xtokA[:, b, :], in_=x_rows[tt], inc="dc")
            def vfence():
                p.wait("vector", "dve", p.val("dve"))
            p.wait("vector", "dc", dv)
            p.op("vector", "reduce_sum", colsA[:, 0:1], xtokA[:, b, :], axis=AX, inc="dve")
            p.wait("scalar", "dc", dv)
            guard_wait("scalar", ("sqA", b))
            p.op("scalar", "activation", sqA[:, b, :], xtokA[:, b, :], AF.Square, inc="act")
            p.wait("vector", "act", p.val("act"))
            p.op("vector", "reduce_sum", colsA[:, 1:2], sqA[:, b, :], axis=AX, inc="dve")
            vfence()
            p.op("vector", "tensor_scalar_mul", colsA[:, 2:3], colsA[:, 0:1], 1.0 / D)
            p.op("vector", "tensor_scalar_mul", colsA[:, 3:4], colsA[:, 1:2], 1.0 / D, inc="dve")
            vfence()
            p.op("vector", "tensor_mul", colsA[:, 4:5], colsA[:, 2:3], colsA[:, 2:3], inc="dve")
            vfence()
            p.op("vector", "tensor_sub", colsA[:, 5:6], colsA[:, 3:4], colsA[:, 4:5], inc="dve")
            vfence()
            p.op("vector", "tensor_scalar", colsA[:, 5:6], colsA[:, 5:6], EPS_LN, None, ALU.add, inc="dve")
            p.wait("scalar", "dve", p.val("dve"))
            p.op("scalar", "activation", colsA[:, 6:7], colsA[:, 5:6], AF.Sqrt, inc="act")
            p.wait("vector", "act", p.val("act"))
            p.op("vector", "reciprocal", colsA[:, 7:8], colsA[:, 6:7], inc="dve")
            vfence()
            p.op("vector", "scalar_tensor_tensor", colsA[:, 8:9], colsA[:, 2:3], -1.0,
                 colsA[:, 7:8], ALU.mult, ALU.mult, inc="dve")
            vfence()
            guard_wait("vector", ("xntokA", b))
            p.op("vector", "tensor_scalar", xntokA[:, b, :], xtokA[:, b, :],
                 colsA[:, 7:8], colsA[:, 8:9], ALU.mult, ALU.add, inc="dve")
            v_xn = p.val("dve")
            guards[("xtokA", b)] = ("dve", v_xn)
            guards[("sqA", b)] = ("dve", v_xn)
            p.wait("scalar", "dve", v_xn)
            guard_wait("scalar", ("xnbfA", b))
            p.op("scalar", "activation", xnbfA[:, b, :], xntokA[:, b, :], AF.Copy, inc="act")
            v_xnbf = p.val("act")
            p.wait("tensor", "act", v_xnbf)
            guard_wait("tensor", "psx")
            for c in range(DC):
                p.op("tensor", "transpose", psx[:, c * 128:(c + 1) * 128],
                     xnbfA[:, b, c * 128:(c + 1) * 128], ident_b[:])
            v_trb = pe_flush()
            p.wait("tensor", "dve", v_xn)
            for half in range(2):
                guard_wait("tensor", ("psa", half))
                for c4 in range(4):
                    c = half * 4 + c4
                    p.op("tensor", "transpose", psa[:, half, c4 * 128:(c4 + 1) * 128],
                         xntokA[:, b, c * 128:(c + 1) * 128], ident_f[:])
            v_trf = pe_flush()
            p.wait("vector", "pe", v_trb)
            p.op("vector", "tensor_copy",
                 xn_fm[:, :, tt * 128:(tt + 1) * 128],
                 psx[:].rearrange("p (c n) -> p c n", c=DC), inc="dve")
            guards["psx"] = ("dve", p.val("dve"))
            guards[("xnbfA", b)] = ("pe", v_trb)
            p.wait("vector", "pe", v_trf)
            for half in range(2):
                p.op("vector", "tensor_copy",
                     result[:, half * 4:(half + 1) * 4, tt * 128:(tt + 1) * 128],
                     psa[:, half, :].rearrange("p (c n) -> p c n", c=4), inc="dve")
                guards[("psa", half)] = ("dve", p.val("dve"))
            guards[("xntokA", b)] = ("pe", v_trf)
        vA_dve = p.val("dve")

        barrier()

        if "resA" in debug:
            p.op("gpsimd", "dma_start", out=dbg_d["res"], in_=result[:], inc="dg")

        for u in range(NU):
            p.op("sync", "dma_start", out=ebias1[:, u * 32:(u + 1) * 32],
                 in_=b1_d[u].rearrange("(c p) -> p c", p=128),
                 allow_slow_non_contiguous=True, inc="dc")
            p.op("sync", "dma_start", out=ebias2[:, u * 8:(u + 1) * 8],
                 in_=b2_d[u].rearrange("(c p) -> p c", p=128),
                 allow_slow_non_contiguous=True, inc="dc")
        p.op("sync", "dma_start", out=gatebp[0:8, :],
             in_=gb_d.rearrange("(p a) -> p a", a=1),
             allow_slow_non_contiguous=True, inc="dc")
        v_bias_dma = p.val("dc")
        p.wait("vector", "dc", v_bias_dma)
        p.wait("scalar", "dc", v_bias_dma)
        dvgw = p.op("sync", "dma_start", out=gate_st[:],
                    in_=gw_d.rearrange("(c p) e -> p c e", p=128), inc="dc")
        gate_sb = wstage[0:8, 0, :]
        p.wait("tensor", "dc", dvgw)
        for tb in range(TBN):
            bank = tb % 2
            guard_wait("tensor", ("psa", bank))
            for c in range(DC):
                p.op("tensor", "matmul", psa[0:8, bank, :], gate_st[:, c, :],
                     result[:, c, tb * NT:(tb + 1) * NT],
                     start=(c == 0), stop=(c == DC - 1),
                     inc="pe" if c == DC - 1 else None)
            v = p.val("pe")
            p.wait("vector", "pe", v)
            guard_wait("vector", "wrowstage")
            p.op("vector", "tensor_scalar", gate_sb[:, tb * NT:(tb + 1) * NT],
                 psa[0:8, bank, :], gatebp[0:8, :], None, ALU.add, inc="dve")
            guards[("psa", bank)] = ("dve", p.val("dve"))
        v_gate = p.val("dve")
        if "router2" in debug:
            p.wait("gpsimd", "dve", v_gate)
            p.op("gpsimd", "dma_start", out=dbg_d["gbp"], in_=gatebp[0:8, :], inc="dg")
            p.op("gpsimd", "dma_start", out=dbg_d["gsb"], in_=gate_sb, inc="dg")
        p.wait("tensor", "dve", v_gate)
        for tt in range(NTT):
            bank = 2 + (tt % 2)
            guard_wait("tensor", ("psa", bank))
            p.op("tensor", "transpose", psa[0:128, bank, 0:8],
                 gate_sb[:, tt * 128:(tt + 1) * 128], ident_f[0:8, 0:8])
            v = pe_flush()
            p.wait("vector", "pe", v)
            p.op("vector", "tensor_copy", rtok[:, tt * 8:(tt + 1) * 8], psa[:, bank, 0:8], inc="dve")
            guards[("psa", bank)] = ("dve", p.val("dve"))
        for tt in range(NTT):
            r8 = rtok[:, tt * 8:(tt + 1) * 8]
            w8 = wtok[:, tt * 8:(tt + 1) * 8]
            c = rtk
            def vf():
                p.wait("vector", "dve", p.val("dve"))
            p.op("vector", "reduce_max", c[:, 0:1], r8, axis=AX, negate=True, inc="dve")
            p.wait("scalar", "dve", p.val("dve"))
            p.op("scalar", "activation", w8, r8, AF.Exp, bias=c[:, 0:1], inc="act")
            p.wait("vector", "act", p.val("act"))
            p.op("vector", "reduce_sum", c[:, 1:2], w8, axis=AX, inc="dve")
            vf()
            p.op("vector", "reciprocal", c[:, 2:3], c[:, 1:2], inc="dve")
            vf()
            p.op("vector", "tensor_scalar", r8, w8, c[:, 2:3], None, ALU.mult, inc="dve")
            vf()
            p.op("vector", "reduce_max", c[:, 3:4], r8, axis=AX, inc="dve")
            vf()
            p.op("vector", "tensor_scalar", c[:, 8:16], r8, c[:, 3:4], None, ALU.is_equal, inc="dve")
            vf()
            p.op("vector", "scalar_tensor_tensor", w8, c[:, 8:16], -2.0, r8, ALU.mult, ALU.add, inc="dve")
            vf()
            p.op("vector", "reduce_max", c[:, 4:5], w8, axis=AX, inc="dve")
            vf()
            p.op("vector", "tensor_scalar", w8, r8, c[:, 4:5], None, ALU.is_ge, inc="dve")
            p.op("vector", "tensor_add", c[:, 5:6], c[:, 3:4], c[:, 4:5], inc="dve")
            vf()
            p.op("vector", "tensor_scalar", c[:, 5:6], c[:, 5:6], EPS_RT, None, ALU.add, inc="dve")
            vf()
            p.op("vector", "reciprocal", c[:, 6:7], c[:, 5:6], inc="dve")
            p.op("vector", "tensor_mul", w8, w8, r8, inc="dve")
            vf()
            p.op("vector", "tensor_scalar", w8, w8, c[:, 6:7], None, ALU.mult, inc="dve")
        v_wtok = p.val("dve")
        if "router2" in debug:
            p.wait("gpsimd", "dve", v_wtok)
            p.op("gpsimd", "dma_start", out=dbg_d["rtok"], in_=rtok, inc="dg")
            p.op("gpsimd", "dma_start", out=dbg_d["wtok"], in_=wtok, inc="dg")
        p.wait("tensor", "dve", v_wtok)
        for tt in range(NTT):
            bank = 2 + (tt % 2)
            guard_wait("tensor", ("psa", bank))
            p.op("tensor", "transpose", psa[0:8, bank, 0:128],
                 wtok[:, tt * 8:(tt + 1) * 8], ident_f[:])
            v = pe_flush()
            p.wait("vector", "pe", v)
            p.op("vector", "tensor_copy", wet[:, tt * 128:(tt + 1) * 128],
                 psa[0:8, bank, 0:128], inc="dve")
            guards[("psa", bank)] = ("dve", p.val("dve"))
        v_wet = p.val("dve")
        if "router" in debug:
            p.wait("gpsimd", "dve", v_wet)
            p.op("gpsimd", "dma_start", out=dbg_d["wet"], in_=wet, inc="dg")

        if phases >= 2:
            barrier()

            wbc_pool_guard = {}

            def build_wbc(e):
                guard_wait("gpsimd", "wrow")
                dv = p.op("gpsimd", "dma_start", out=wrow, in_=wet[e:e + 1, :], inc="dg")
                p.wait("tensor", "dg", dv)
                pg = wbc_pool_guard.get(e % 2)
                if pg:
                    p.wait("vector", "pool", pg)
                for tb in range(TBN):
                    guard_wait("tensor", ("psa", 6))
                    p.op("tensor", "matmul", psa[:, 6, :], ones_row[:],
                         wrow[:, tb * NT:(tb + 1) * NT], start=True, stop=True, inc="pe")
                    v = p.val("pe")
                    guards["wrow"] = ("pe", v)
                    p.wait("vector", "pe", v)
                    guard_wait("vector", ("wbc", e % 2))
                    p.op("vector", "tensor_copy", wbc[:, e % 2, tb * NT:(tb + 1) * NT],
                         psa[:, 6, :], inc="dve")
                    guards[("psa", 6)] = ("dve", p.val("dve"))
                return p.val("dve")

            wstage_slot = [0]

            def dma_w1(u, ig):
                dvs = []
                for half in range(2):
                    slot = wstage_slot[0]; wstage_slot[0] ^= 1
                    guard_wait("sync", ("wstage", slot))
                    src = w1_d[u].rearrange("(c p) i -> p c i", p=128)[
                        :, :, ig * 512 + half * 256: ig * 512 + (half + 1) * 256]
                    dv = p.op("sync", "dma_start",
                              out=wstage[:, slot, :].rearrange("p (c i) -> p c i", c=DC),
                              in_=src, inc="ds")
                    dvs.append((slot, dv))
                return dvs

            def conv_w1(ig, dvs):
                buf = ig % 2
                guard_wait("scalar", ("w1bf", buf))
                for half, (slot, dv) in enumerate(dvs):
                    p.wait("scalar", "ds", dv)
                    p.op("scalar", "activation",
                         w1bf[:, buf, :, half * 256:(half + 1) * 256],
                         wstage[:, slot, :].rearrange("p (c i) -> p c i", c=DC), AF.Copy, inc="act")
                    guards[("wstage", slot)] = ("act", p.val("act"))
                return p.val("act")

            def dma_w2(u, ig):
                dvs = []
                for half in range(2):
                    slot = wstage_slot[0]; wstage_slot[0] ^= 1
                    guard_wait("sync", ("wstage", slot))
                    src = w2_d[u].rearrange("(g p) d -> p g d", p=128)[
                        :, ig * ICG + half * 2: ig * ICG + (half + 1) * 2, :]
                    dv = p.op("sync", "dma_start",
                              out=wstage[:, slot, :].rearrange("p (g d) -> p g d", g=2),
                              in_=src, inc="ds")
                    dvs.append((slot, dv))
                return dvs

            def conv_w2(ig, dvs):
                buf = ig % 2
                guard_wait("scalar", ("w2bf", buf))
                for half, (slot, dv) in enumerate(dvs):
                    p.wait("scalar", "ds", dv)
                    p.op("scalar", "activation",
                         w2bf[:, buf, half * 2:(half + 1) * 2, :],
                         wstage[:, slot, :].rearrange("p (j d) -> p j d", j=2), AF.Copy, inc="act")
                    guards[("wstage", slot)] = ("act", p.val("act"))
                return p.val("act")

            wbc_ready = {0: build_wbc(0)}
            d1 = dma_w1(0, 0)
            w1_ready = conv_w1(0, d1)
            d2 = dma_w2(0, 0)

            gelu_j = [0]
            units = [(u, ig) for u in range(NU) for ig in range(NIG)]
            for idx, (u, ig) in enumerate(units):
                is_exp = u >= 1
                e = u - 1
                buf = ig % 2
                if is_exp and ig == 0:
                    p.wait("gpsimd", "dve", wbc_ready[e])
                w2_ready = conv_w2(ig, d2)
                nxt = units[idx + 1] if idx + 1 < len(units) else None
                if nxt:
                    d1 = dma_w1(*nxt)
                p.wait("tensor", "act", w1_ready)
                h_done = {}
                for ic in range(ICG):
                    icg = ig * ICG + ic
                    for tb in range(TBN):
                        bank = (ic * TBN + tb) % 3
                        guard_wait("tensor", ("psa", bank))
                        for c in range(DC):
                            p.op("tensor", "matmul", psa[:, bank, :],
                                 w1bf[:, buf, c, ic * 128:(ic + 1) * 128],
                                 xn_fm[:, c, tb * NT:(tb + 1) * NT],
                                 start=(c == 0), stop=(c == DC - 1),
                                 inc="pe" if c == DC - 1 else None)
                        v_h = p.val("pe")
                        guards[("w1bf", buf)] = ("pe", v_h)
                        p.wait("scalar", "pe", v_h)
                        bias_ap = ebias1[:, u * 32 + icg:u * 32 + icg + 1]
                        if is_exp:
                            j = gelu_j[0] % 2
                            gelu_j[0] += 1
                            guard_wait("scalar", ("g1", j))
                            p.op("scalar", "activation", g1[:, j, :], psa[:, bank, :],
                                 AF.Gelu_apprx_tanh, bias=bias_ap, inc="act")
                            v_gl = p.val("act")
                            guards[("psa", bank)] = ("act", v_gl)
                            p.wait("gpsimd", "act", v_gl)
                            guard_wait("gpsimd", "h_sb")
                            p.op("gpsimd", "tensor_mul", h_sb[:, ic, tb * NT:(tb + 1) * NT],
                                 g1[:, j, :], wbc[:, e % 2, tb * NT:(tb + 1) * NT], inc="pool")
                            guards[("g1", j)] = ("pool", p.val("pool"))
                            h_done[(ic, tb)] = ("pool", p.val("pool"))
                        else:
                            guard_wait("scalar", "h_sb")
                            p.op("scalar", "activation", h_sb[:, ic, tb * NT:(tb + 1) * NT],
                                 psa[:, bank, :], AF.Gelu_apprx_tanh, bias=bias_ap, inc="act")
                            guards[("psa", bank)] = ("act", p.val("act"))
                            h_done[(ic, tb)] = ("act", p.val("act"))
                if is_exp:
                    wbc_pool_guard[e % 2] = p.val("pool")
                if nxt:
                    w1_ready = conv_w1(nxt[1], d1)
                    d2 = dma_w2(*nxt)
                if ig == 0 and u + 1 < NU:
                    wbc_ready[u] = build_wbc(u)
                p.wait("tensor", "act", w2_ready)
                for dcc in range(DC):
                    for tb in range(TBN):
                        bank = 3 + (dcc * TBN + tb) % 3
                        guard_wait("tensor", ("psa", bank))
                        sem_h, v_hh = h_done[(ICG - 1, tb)]
                        p.wait("tensor", sem_h, v_hh)
                        for j in range(ICG):
                            p.op("tensor", "matmul", psa[:, bank, :],
                                 w2bf[:, buf, j, dcc * 128:(dcc + 1) * 128],
                                 h_sb[:, j, tb * NT:(tb + 1) * NT],
                                 start=(j == 0), stop=(j == ICG - 1),
                                 inc="pe" if j == ICG - 1 else None)
                        v_o = p.val("pe")
                        guards[("w2bf", buf)] = ("pe", v_o)
                        p.wait("vector", "pe", v_o)
                        p.op("vector", "tensor_tensor",
                             result[:, dcc, tb * NT:(tb + 1) * NT],
                             result[:, dcc, tb * NT:(tb + 1) * NT],
                             psa[:, bank, :], ALU.add, inc="dve")
                        guards[("psa", bank)] = ("dve", p.val("dve"))
                guards["h_sb"] = ("pe", p.val("pe"))
                if ig == NIG - 1:
                    if is_exp:
                        for dcc in range(DC):
                            p.op("vector", "scalar_tensor_tensor",
                                 result[:, dcc, :], wbc[:, e % 2, :],
                                 ebias2[:, u * 8 + dcc:u * 8 + dcc + 1],
                                 result[:, dcc, :], ALU.mult, ALU.add, inc="dve")
                        guards[("wbc", e % 2)] = ("dve", p.val("dve"))
                    else:
                        for dcc in range(DC):
                            p.op("vector", "tensor_scalar", result[:, dcc, :], result[:, dcc, :],
                                 ebias2[:, u * 8 + dcc:u * 8 + dcc + 1], None, ALU.add, inc="dve")
            v_ffn = p.val("dve")
            if "ffn" in debug:
                p.wait("gpsimd", "dve", v_ffn)
                p.op("gpsimd", "dma_start", out=dbg_d["res"], in_=result[:], inc="dg")

        if phases >= 3:
            barrier()

            for dcc in range(DC):
                p.op("scalar", "activation", result[:, dcc, :], result[:, dcc, :],
                     AF.Silu, inc="act")
            v_g = p.val("act")
            if "g" in debug:
                p.wait("gpsimd", "act", v_g)
                p.op("gpsimd", "dma_start", out=dbg_d["res"], in_=result[:], inc="dg")

            def stream_qkv(src_ap, slot):
                guard_wait("sync", ("wqkv_st", slot))
                dv = p.op("sync", "dma_start", out=wqkv_st[:, slot, :, :],
                          in_=src_ap.rearrange("(c p) k -> p c k", p=128), inc="ds")
                p.wait("scalar", "ds", dv)
                guard_wait("scalar", ("wqkv_bf", slot))
                p.op("scalar", "activation",
                     wqkv_bf[:, slot, :, :].rearrange("p c k -> p (c k)"),
                     wqkv_st[:, slot, :, :].rearrange("p c k -> p (c k)"), AF.Copy, inc="act")
                guards[("wqkv_st", slot)] = ("act", p.val("act"))
                return p.val("act")

            def proj_2048(slot, dst, vready, banks=(0, 1)):
                p.wait("tensor", "act", vready)
                for nb in range(TBN):
                    bank = banks[nb % len(banks)]
                    guard_wait("tensor", ("psa", bank))
                    for c in range(DC):
                        p.op("tensor", "matmul", psa[:, bank, :],
                             wqkv_bf[:, slot, c, :],
                             xn_fm[:, c, nb * NT:(nb + 1) * NT],
                             start=(c == 0), stop=(c == DC - 1),
                             inc="pe" if c == DC - 1 else None)
                    v = p.val("pe")
                    guards[("wqkv_bf", slot)] = ("pe", v)
                    p.wait("vector", "pe", v)
                    p.op("vector", "tensor_copy", dst[:, nb * NT:(nb + 1) * NT],
                         psa[:, bank, :], inc="dve")
                    guards[("psa", bank)] = ("dve", p.val("dve"))
                return p.val("dve")

            v_ksb = proj_2048(0, k_sb, stream_qkv(wk_d, 0))
            v_vsb = proj_2048(1, v_sb, stream_qkv(wv_d, 1))
            p.wait("tensor", "dve", v_vsb)
            for rnd in range(2):
                guard_wait("tensor", "psx")
                for t8 in range(8):
                    mc = rnd * 8 + t8
                    p.op("tensor", "transpose", psx[:, t8 * 128:(t8 + 1) * 128],
                         v_sb[:, mc * 128:(mc + 1) * 128], ident_b[:])
                v = pe_flush()
                p.wait("vector", "pe", v)
                p.op("vector", "tensor_copy", vt_sb[:, rnd * 1024:(rnd + 1) * 1024],
                     psx[:], inc="dve")
                guards["psx"] = ("dve", p.val("dve"))
            v_vt_pe = p.val("pe")
            v_vt = p.val("dve")
            if "kv" in debug:
                p.wait("gpsimd", "dve", v_vt)
                p.op("gpsimd", "dma_start", out=dbg_d["kv"][:, 0, :], in_=k_sb, inc="dg")
                p.op("gpsimd", "dma_start", out=dbg_d["kv"][:, 1, :], in_=vt_sb, inc="dg")

            def proj_q(h):
                slot = h % 2
                vr = stream_qkv(wq_d[h], slot)
                p.wait("tensor", "act", vr)
                for nb in range(TBN):
                    guard_wait("tensor", ("psa", 5))
                    for c in range(DC):
                        p.op("tensor", "matmul", psa[:, 5, :],
                             wqkv_bf[:, slot, c, :],
                             xn_fm[:, c, nb * NT:(nb + 1) * NT],
                             start=(c == 0), stop=(c == DC - 1),
                             inc="pe" if c == DC - 1 else None)
                    v = p.val("pe")
                    guards[("wqkv_bf", slot)] = ("pe", v)
                    p.wait("vector", "pe", v)
                    guard_wait("vector", ("q_sb", h % 2))
                    p.op("vector", "tensor_copy", q_sb[:, h % 2, nb * NT:(nb + 1) * NT],
                         psa[:, 5, :], inc="dve")
                    guards[("psa", 5)] = ("dve", p.val("dve"))
                return p.val("dve")

            guards[("q_sb", 0)] = ("pe", v_vt_pe)
            q_ready = proj_q(0)

            for h in range(NH):
                p.wait("tensor", "dve", q_ready)
                for nb in range(TBN):
                    otbank = 2 + nb % 2
                    guard_wait("tensor", ("psa", otbank), ("psa", 4))
                    for mc in range(16):
                        ltbank = mc % 2
                        guard_wait("tensor", ("psa", ltbank))
                        p.op("tensor", "matmul", psa[:, ltbank, :],
                             k_sb[:, mc * 128:(mc + 1) * 128],
                             q_sb[:, h % 2, nb * NT:(nb + 1) * NT],
                             start=True, stop=True, inc="pe")
                        v_l = p.val("pe")
                        p.wait("scalar", "pe", v_l)
                        guard_wait("scalar", ("e_sb", mc % 2))
                        p.op("scalar", "activation", e_sb[:, mc % 2, :], psa[:, ltbank, :],
                             AF.Exp, inc="act")
                        v_e = p.val("act")
                        guards[("psa", ltbank)] = ("act", v_e)
                        p.wait("tensor", "act", v_e)
                        p.op("tensor", "matmul", psa[:, otbank, :],
                             vt_sb[:, mc * 128:(mc + 1) * 128], e_sb[:, mc % 2, :],
                             start=(mc == 0), stop=(mc == 15))
                        p.op("tensor", "matmul", psa[0:1, 4, :],
                             ones_col[:], e_sb[:, mc % 2, :],
                             start=(mc == 0), stop=(mc == 15), inc="pe")
                        guards[("e_sb", mc % 2)] = ("pe", p.val("pe"))
                    v_osum = p.val("pe")
                    p.wait("vector", "pe", v_osum)
                    guard_wait("vector", "sumrec")
                    p.op("vector", "reciprocal", sumrec, psa[0:1, 4, :], inc="dve")
                    v_sr = p.val("dve")
                    guards[("psa", 4)] = ("dve", v_sr)
                    p.wait("tensor", "dve", v_sr)
                    guard_wait("tensor", ("psa", 6))
                    p.op("tensor", "matmul", psa[:, 6, :], ones_row[:], sumrec,
                         start=True, stop=True, inc="pe")
                    v_rb = p.val("pe")
                    guards["sumrec"] = ("pe", v_rb)
                    p.wait("vector", "pe", v_rb)
                    p.op("vector", "tensor_copy", rb_sb[:, nb % 2, :], psa[:, 6, :], inc="dve")
                    guards[("psa", 6)] = ("dve", p.val("dve"))
                    p.wait("vector", "dve", p.val("dve"))
                    p.op("vector", "tensor_tensor", ocat[:, h, nb * NT:(nb + 1) * NT],
                         psa[:, otbank, :], rb_sb[:, nb % 2, :], ALU.mult, inc="dve")
                    guards[("psa", otbank)] = ("dve", p.val("dve"))
                guards[("q_sb", h % 2)] = ("pe", p.val("pe"))
                if h + 1 < NH:
                    q_ready = proj_q(h + 1)
            v_attn_pe = p.val("pe")
            v_ocat = p.val("dve")
            if "ocat" in debug:
                p.wait("gpsimd", "dve", v_ocat)
                p.op("gpsimd", "dma_start", out=dbg_d["ocat"],
                     in_=ocat.rearrange("p h n -> p (h n)"), inc="dg")

            wo_slot = [0]
            for hh in range(NH):
                slot = wo_slot[0]; wo_slot[0] ^= 1
                guard_wait("sync", ("wo_st", slot))
                dv = p.op("sync", "dma_start", out=wo_st[:, slot, :], in_=wot_d[hh], inc="ds")
                p.wait("scalar", "ds", dv)
                if hh == 0:
                    p.wait("scalar", "pe", v_attn_pe)
                p.op("scalar", "activation", wo_bf[:, hh, :], wo_st[:, slot, :], AF.Copy, inc="act")
                guards[("wo_st", slot)] = ("act", p.val("act"))
            v_wo = p.val("act")

            out_rows = out_d.rearrange("(t p) d -> t p d", p=128)
            xtok_dma = {}
            for ns in range(2):
                xtok_dma[ns] = p.op("sync", "dma_start", out=xtokC[:, ns % 2, :],
                                    in_=x_rows[ns], inc="ds")
            p.wait("tensor", "act", v_wo)
            p.wait("tensor", "dve", v_ocat)
            for ns in range(NTT):
                yb = 2 * (ns % 2)
                gbank = 4 + ns % 2
                guard_wait("tensor", ("psa", yb), ("psa", yb + 1))
                for hh in range(NH):
                    for half in range(2):
                        p.op("tensor", "matmul", psa[:, yb + half, :],
                             ocat[:, hh, ns * 128:(ns + 1) * 128],
                             wo_bf[:, hh, half * 512:(half + 1) * 512],
                             start=(hh == 0), stop=(hh == NH - 1),
                             inc="pe" if (hh == NH - 1 and half == 1) else None)
                v_y = p.val("pe")
                for half in range(2):
                    guard_wait("tensor", ("psa", gbank))
                    for c4 in range(4):
                        c = half * 4 + c4
                        p.op("tensor", "transpose", psa[:, gbank, c4 * 128:(c4 + 1) * 128],
                             result[:, c, ns * 128:(ns + 1) * 128], ident_f[:])
                    v_t = pe_flush()
                    p.wait("vector", "pe", v_t)
                    guard_wait("vector", ("gt", ns % 2))
                    p.op("vector", "tensor_copy",
                         gt_sb[:, ns % 2, half * 512:(half + 1) * 512],
                         psa[:, gbank, :], inc="dve")
                    guards[("psa", gbank)] = ("dve", p.val("dve"))
                p.wait("vector", "pe", v_y)
                p.wait("vector", "ds", xtok_dma[ns])
                p.wait("vector", "dve", p.val("dve"))
                yview = psa[:, yb:yb + 2, :].rearrange("p a b -> p (a b)")
                p.op("vector", "tensor_tensor", gt_sb[:, ns % 2, :], yview,
                     gt_sb[:, ns % 2, :], ALU.mult, inc="dve")
                p.wait("vector", "dve", p.val("dve"))
                p.op("vector", "tensor_tensor", xtokC[:, ns % 2, :], xtokC[:, ns % 2, :],
                     gt_sb[:, ns % 2, :], ALU.add, inc="dve")
                v_ep = p.val("dve")
                guards[("psa", yb)] = ("dve", v_ep)
                guards[("psa", yb + 1)] = ("dve", v_ep)
                guards[("gt", ns % 2)] = ("dve", v_ep)
                p.wait("sync", "dve", v_ep)
                p.op("sync", "dma_start", out=out_rows[ns], in_=xtokC[:, ns % 2, :], inc="dc")
                guards[("xtokC", ns % 2)] = ("dc", p.val("dc"))
                if ns + 2 < NTT:
                    guard_wait("sync", ("xtokC", ns % 2))
                    xtok_dma[ns + 2] = p.op("sync", "dma_start", out=xtokC[:, ns % 2, :],
                                            in_=x_rows[ns + 2], inc="ds")
            p.wait("scalar", "dc", p.val("dc"))
            if debug:
                p.wait("gpsimd", "dg", p.val("dg"))
        else:
            if debug:
                p.wait("gpsimd", "dg", p.val("dg"))
            p.wait("gpsimd", "dve", p.val("dve"))
            p.op("gpsimd", "dma_start",
                 out=out_d.rearrange("(c p) d -> p c d", p=128),
                 in_=result[:].rearrange("p c n -> p (c n)").rearrange("p (a b) -> p a b", a=16),
                 inc="dg")
            p.wait("gpsimd", "dg", p.val("dg"))

        p.check()

        with nc.Block() as block:
            @block.tensor
            def _(h):
                for f in p.ops["tensor"]:
                    f(h, sems)

            @block.vector
            def _(h):
                for f in p.ops["vector"]:
                    f(h, sems)

            @block.scalar
            def _(h):
                for f in p.ops["scalar"]:
                    f(h, sems)

            @block.gpsimd
            def _(h):
                for f in p.ops["gpsimd"]:
                    f(h, sems)

            @block.sync
            def _(h):
                for f in p.ops["sync"]:
                    f(h, sems)

    return nc


def make_in_maps(L0_attn_x, L0_attn_proj_q, L0_attn_proj_k, L0_attn_proj_v, L0_attn_proj_o,
                 L0_ffn_gate_w, L0_ffn_gate_b, L0_ffn_sh_w1, L0_ffn_sh_b1, L0_ffn_sh_w2,
                 L0_ffn_sh_b2, L0_ffn_ew1, L0_ffn_eb1, L0_ffn_ew2, L0_ffn_eb2):
    f32 = np.float32
    common = {
        "wq": np.ascontiguousarray(L0_attn_proj_q, f32),
        "wk": np.ascontiguousarray(L0_attn_proj_k, f32),
        "wv": np.ascontiguousarray(L0_attn_proj_v, f32),
        "wot": np.ascontiguousarray(np.asarray(L0_attn_proj_o, f32).transpose(0, 2, 1)),
        "gw": np.ascontiguousarray(L0_ffn_gate_w, f32),
        "gb": np.ascontiguousarray(L0_ffn_gate_b, f32),
        "shw1": np.ascontiguousarray(L0_ffn_sh_w1, f32),
        "shb1": np.ascontiguousarray(L0_ffn_sh_b1, f32),
        "shw2": np.ascontiguousarray(L0_ffn_sh_w2, f32),
        "shb2": np.ascontiguousarray(L0_ffn_sh_b2, f32),
        "ew1": np.ascontiguousarray(L0_ffn_ew1, f32),
        "eb1": np.ascontiguousarray(L0_ffn_eb1, f32),
        "ew2": np.ascontiguousarray(L0_ffn_ew2, f32),
        "eb2": np.ascontiguousarray(L0_ffn_eb2, f32),
    }
    xs = np.ascontiguousarray(L0_attn_x, f32)
    return [dict(common, x=xs[c]) for c in range(8)]


def kernel(**inputs):
    nc = build_nc()
    in_maps = make_in_maps(**inputs)
    res = run_bass_kernel_spmd(nc, in_maps, core_ids=list(range(8)))
    return np.stack([res.results[c]["out"] for c in range(8)], axis=0)


# version 04
# speedup vs baseline: 1.3055x; 1.1726x over previous
import contextlib
import numpy as np
import concourse.bass as bass
import concourse.mybir as mybir
from concourse.bass_utils import run_bass_kernel_spmd

F32 = mybir.dt.float32
BF16 = mybir.dt.bfloat16
AF = mybir.ActivationFunctionType
ALU = mybir.AluOpType
AX = mybir.AxisListType.X

S = 2048
D = 1024
DC = 8
NH = 8
FI = 4096
NE = 8
NU = NE + 1
NTT = 16
NT = 512
TBN = 4
NIG = 8
ICG = 4
EPS_LN = 1e-5
EPS_RT = 1e-9

ENGS = ("tensor", "vector", "scalar", "gpsimd", "sync")
DMA_SEMS = {"ds", "dc", "dg", "ds0", "ds1", "dc0", "dc1", "dgb"}
ALL_SEMS = ("pe", "act", "dve", "pool", "ds", "dc", "dg",
            "ds0", "ds1", "dc0", "dc1", "dgb")


class Plan:

    def __init__(self):
        self.ops = {e: [] for e in ENGS}
        self.cnt = {}
        self.waits = []

    def op(self, eng, meth, *args, inc=None, **kwargs):
        step = 16 if inc in DMA_SEMS else 1
        if inc is not None:
            self.cnt[inc] = self.cnt.get(inc, 0) + step
        v = self.cnt.get(inc) if inc is not None else None

        def f(h, sems, meth=meth, args=args, kwargs=kwargs, inc=inc, step=step):
            ins = getattr(h, meth)(*args, **kwargs)
            if inc is not None:
                ins.then_inc(sems[inc], step)
        self.ops[eng].append(f)
        return v

    def wait(self, eng, sem, val):
        if val is None or val <= 0:
            return
        self.waits.append((sem, val))

        def f(h, sems, sem=sem, val=val):
            h.wait_ge(sems[sem], val)
        self.ops[eng].append(f)

    def val(self, sem):
        return self.cnt.get(sem, 0)

    def check(self):
        for sem, val in self.waits:
            assert val <= self.cnt.get(sem, 0), \
                f"wait {sem}>={val} never satisfied (final {self.cnt.get(sem, 0)})"


def build_nc(phases=3, debug=()):
    nc = bass.Bass("TRN2", target_bir_lowering=False)
    debug = set(debug)

    ap = lambda t: t.ap()
    x_d = ap(nc.dram_tensor("x", [S, D], F32, kind="ExternalInput"))
    wq_d = ap(nc.dram_tensor("wq", [NH, D, 128], F32, kind="ExternalInput"))
    wk_d = ap(nc.dram_tensor("wk", [D, 128], F32, kind="ExternalInput"))
    wv_d = ap(nc.dram_tensor("wv", [D, 128], F32, kind="ExternalInput"))
    wot_d = ap(nc.dram_tensor("wot", [NH, 128, D], F32, kind="ExternalInput"))
    gw_d = ap(nc.dram_tensor("gw", [D, NE], F32, kind="ExternalInput"))
    gb_d = ap(nc.dram_tensor("gb", [NE], F32, kind="ExternalInput"))
    w1_d = [ap(nc.dram_tensor("shw1", [D, FI], F32, kind="ExternalInput"))]
    b1_d = [ap(nc.dram_tensor("shb1", [FI], F32, kind="ExternalInput"))]
    w2_d = [ap(nc.dram_tensor("shw2", [FI, D], F32, kind="ExternalInput"))]
    b2_d = [ap(nc.dram_tensor("shb2", [D], F32, kind="ExternalInput"))]
    ew1_d = ap(nc.dram_tensor("ew1", [NE, D, FI], F32, kind="ExternalInput"))
    eb1_d = ap(nc.dram_tensor("eb1", [NE, FI], F32, kind="ExternalInput"))
    ew2_d = ap(nc.dram_tensor("ew2", [NE, FI, D], F32, kind="ExternalInput"))
    eb2_d = ap(nc.dram_tensor("eb2", [NE, D], F32, kind="ExternalInput"))
    for e in range(NE):
        w1_d.append(ew1_d[e])
        b1_d.append(eb1_d[e])
        w2_d.append(ew2_d[e])
        b2_d.append(eb2_d[e])
    out_d = ap(nc.dram_tensor("out", [S, D], F32, kind="ExternalOutput"))
    dbg_d = {}
    if debug & {"resA", "ffn", "g"}:
        dbg_d["res"] = ap(nc.dram_tensor("dbg_res", [128, DC, S], F32, kind="ExternalOutput"))
    if "router" in debug:
        dbg_d["wet"] = ap(nc.dram_tensor("dbg_wet", [NE, S], F32, kind="ExternalOutput"))
    if "router2" in debug:
        dbg_d["gbp"] = ap(nc.dram_tensor("dbg_gbp", [8, 1], F32, kind="ExternalOutput"))
        dbg_d["grow"] = ap(nc.dram_tensor("dbg_grow", [1, 8], F32, kind="ExternalOutput"))
        dbg_d["gsb"] = ap(nc.dram_tensor("dbg_gsb", [8, S], F32, kind="ExternalOutput"))
        dbg_d["rtok"] = ap(nc.dram_tensor("dbg_rtok", [128, 128], F32, kind="ExternalOutput"))
        dbg_d["wtok"] = ap(nc.dram_tensor("dbg_wtok", [128, 128], F32, kind="ExternalOutput"))
    if "ocat" in debug:
        dbg_d["ocat"] = ap(nc.dram_tensor("dbg_ocat", [128, NH * S], BF16, kind="ExternalOutput"))
    if "kv" in debug:
        dbg_d["kv"] = ap(nc.dram_tensor("dbg_kv", [128, 2, S], BF16, kind="ExternalOutput"))

    p = Plan()

    with contextlib.ExitStack() as ctx:
        E = ctx.enter_context
        result = E(nc.sbuf_tensor("result", [128, DC, S], F32))
        xn_fm = E(nc.sbuf_tensor("xn_fm", [128, DC, S], BF16))
        abf = E(nc.sbuf_tensor("abf", [128, 27648], BF16))
        af = E(nc.sbuf_tensor("af", [128, 13312], F32))
        ident_f = E(nc.sbuf_tensor("ident_f", [128, 128], F32))
        ident_b = E(nc.sbuf_tensor("ident_b", [128, 128], BF16))
        ones_col = E(nc.sbuf_tensor("ones_col", [128, 1], BF16))
        ones_row = E(nc.sbuf_tensor("ones_row", [1, 128], F32))
        ebias1 = E(nc.sbuf_tensor("ebias1", [128, NU * 32], F32))
        ebias2 = E(nc.sbuf_tensor("ebias2", [128, NU * 8], F32))
        gatebp = E(nc.sbuf_tensor("gatebp", [128, 1], F32))
        gate_st = E(nc.sbuf_tensor("gate_st", [128, 8, 8], F32))
        rtk = E(nc.sbuf_tensor("rtk", [128, 16], F32))

        bf = abf[:]
        fa = af[:]
        xtokA = fa[:, 0:2048].rearrange("p (b n) -> p b n", b=2)
        xntokA = fa[:, 2048:4096].rearrange("p (b n) -> p b n", b=2)
        sqA = fa[:, 4096:6144].rearrange("p (b n) -> p b n", b=2)
        colsA = fa[:, 6144:6160]
        xnbfA = bf[:, 0:2048].rearrange("p (b n) -> p b n", b=2)
        wstage = fa[:, 0:4096].rearrange("p (b n) -> p b n", b=2)
        wrowstage = fa[0:1, 0:4096]
        wbc = fa[:, 4096:8192].rearrange("p (b n) -> p b n", b=2)
        g1 = fa[:, 8192:9216].rearrange("p (b n) -> p b n", b=2)
        rtok = fa[:, 8192:8320]
        wtok = fa[:, 8320:8448]
        wrow = fa[0:1, 9216:11264]
        wet = fa[0:8, 11264:13312]
        w1bf = bf[:, 0:8192].rearrange("p (b c i) -> p b c i", b=2, c=DC)
        w2bf = bf[:, 8192:16384].rearrange("p (b j d) -> p b j d", b=2, j=ICG)
        h_sb = bf[:, 16384:24576].rearrange("p (j n) -> p j n", j=ICG)
        ocat = bf[:, 0:16384].rearrange("p (h n) -> p h n", h=NH)
        k_sb = bf[:, 16384:18432]
        vt_sb = bf[:, 18432:20480]
        v_sb = bf[:, 20480:22528]
        q_sb = bf[:, 20480:24576].rearrange("p (b n) -> p b n", b=2)
        e_sb = bf[:, 24576:25600].rearrange("p (b n) -> p b n", b=2)
        wqkv_bf = bf[:, 25600:27648].rearrange("p (b c k) -> p b c k", b=2, c=DC)
        wo_bf = bf[:, 16384:24576].rearrange("p (h d) -> p h d", h=NH)
        rb_sb = fa[:, 0:1024].rearrange("p (b n) -> p b n", b=2)
        gt_sb = fa[:, 1024:3072].rearrange("p (b n) -> p b n", b=2)
        xtokC = fa[:, 3072:5120].rearrange("p (b n) -> p b n", b=2)
        wqkv_st = fa[:, 5120:7168].rearrange("p (b c k) -> p b c k", b=2, c=DC)
        wo_st = fa[:, 7168:9216].rearrange("p (b n) -> p b n", b=2)
        sumrec = fa[0:1, 9216:9728]

        psa = E(nc.psum_tensor("psa", [128, 7, 512], F32))
        psx = E(nc.psum_tensor("psx", [128, 1024], BF16))

        sems = {}
        for s_ in ALL_SEMS:
            sems[s_] = E(nc.semaphore(s_))

        guards = {}

        def guard_wait(eng, *keys):
            for key in keys:
                g = guards.get(key)
                if g:
                    p.wait(eng, g[0], g[1])

        def barrier():
            vals = [(s_, p.val(s_)) for s_ in ALL_SEMS]
            for e_ in ENGS:
                for s_, v_ in vals:
                    p.wait(e_, s_, v_)

        def pe_flush():
            p.op("tensor", "matmul", psa[0:128, 6, 256:384], ident_b[:],
                 ident_b[:], start=True, stop=True)
            p.op("tensor", "matmul", psa[0:128, 6, 384:512], ident_b[:],
                 ident_b[:], start=True, stop=True, inc="pe")
            return p.val("pe")

        p.op("gpsimd", "memset", ident_f[:], 0.0)
        p.op("gpsimd", "memset", ident_b[:], 0.0, inc="pool")
        p.wait("gpsimd", "pool", p.val("pool"))
        p.op("gpsimd", "affine_select", out=ident_f[:], in_=ident_f[:],
             compare_op=ALU.not_equal, fill=1.0, base=0, pattern=[[-1, 128]], channel_multiplier=1)
        p.op("gpsimd", "affine_select", out=ident_b[:], in_=ident_b[:],
             compare_op=ALU.not_equal, fill=1.0, base=0, pattern=[[-1, 128]], channel_multiplier=1)
        p.op("gpsimd", "memset", ones_col[:], 1.0)
        p.op("gpsimd", "memset", ones_row[:], 1.0, inc="pool")
        v_const = p.val("pool")
        p.wait("tensor", "pool", v_const)

        for u in range(NU):
            p.op("gpsimd", "dma_start", out=ebias1[:, u * 32:(u + 1) * 32],
                 in_=b1_d[u].rearrange("(c p) -> p c", p=128),
                 allow_slow_non_contiguous=True, inc="dgb")
            p.op("gpsimd", "dma_start", out=ebias2[:, u * 8:(u + 1) * 8],
                 in_=b2_d[u].rearrange("(c p) -> p c", p=128),
                 allow_slow_non_contiguous=True, inc="dgb")
        p.op("gpsimd", "dma_start", out=gatebp[0:8, :],
             in_=gb_d.rearrange("(p a) -> p a", a=1),
             allow_slow_non_contiguous=True, inc="dgb")
        v_bias_dma = p.val("dgb")

        x_rows = x_d.rearrange("(t p) d -> t p d", p=128)
        for tt in range(NTT):
            b = tt % 2
            guard_wait("sync", ("xtokA", b))
            dv = p.op("sync", "dma_start", out=xtokA[:, b, :], in_=x_rows[tt], inc=f"dc{b}")
            def vfence():
                p.wait("vector", "dve", p.val("dve"))
            p.wait("vector", f"dc{b}", dv)
            p.op("vector", "reduce_sum", colsA[:, 0:1], xtokA[:, b, :], axis=AX, inc="dve")
            p.wait("scalar", f"dc{b}", dv)
            guard_wait("scalar", ("sqA", b))
            p.op("scalar", "activation", sqA[:, b, :], xtokA[:, b, :], AF.Square, inc="act")
            p.wait("vector", "act", p.val("act"))
            p.op("vector", "reduce_sum", colsA[:, 1:2], sqA[:, b, :], axis=AX, inc="dve")
            vfence()
            p.op("vector", "tensor_scalar_mul", colsA[:, 2:3], colsA[:, 0:1], 1.0 / D)
            p.op("vector", "tensor_scalar_mul", colsA[:, 3:4], colsA[:, 1:2], 1.0 / D, inc="dve")
            vfence()
            p.op("vector", "tensor_mul", colsA[:, 4:5], colsA[:, 2:3], colsA[:, 2:3], inc="dve")
            vfence()
            p.op("vector", "tensor_sub", colsA[:, 5:6], colsA[:, 3:4], colsA[:, 4:5], inc="dve")
            vfence()
            p.op("vector", "tensor_scalar", colsA[:, 5:6], colsA[:, 5:6], EPS_LN, None, ALU.add, inc="dve")
            p.wait("scalar", "dve", p.val("dve"))
            p.op("scalar", "activation", colsA[:, 6:7], colsA[:, 5:6], AF.Sqrt, inc="act")
            p.wait("vector", "act", p.val("act"))
            p.op("vector", "reciprocal", colsA[:, 7:8], colsA[:, 6:7], inc="dve")
            vfence()
            p.op("vector", "scalar_tensor_tensor", colsA[:, 8:9], colsA[:, 2:3], -1.0,
                 colsA[:, 7:8], ALU.mult, ALU.mult, inc="dve")
            vfence()
            guard_wait("vector", ("xntokA", b))
            p.op("vector", "tensor_scalar", xntokA[:, b, :], xtokA[:, b, :],
                 colsA[:, 7:8], colsA[:, 8:9], ALU.mult, ALU.add, inc="dve")
            v_xn = p.val("dve")
            guards[("xtokA", b)] = ("dve", v_xn)
            guards[("sqA", b)] = ("dve", v_xn)
            p.wait("scalar", "dve", v_xn)
            guard_wait("scalar", ("xnbfA", b))
            p.op("scalar", "activation", xnbfA[:, b, :], xntokA[:, b, :], AF.Copy, inc="act")
            v_xnbf = p.val("act")
            p.wait("tensor", "act", v_xnbf)
            guard_wait("tensor", "psx")
            for c in range(DC):
                p.op("tensor", "transpose", psx[:, c * 128:(c + 1) * 128],
                     xnbfA[:, b, c * 128:(c + 1) * 128], ident_b[:])
            v_trb = pe_flush()
            p.wait("tensor", "dve", v_xn)
            for half in range(2):
                guard_wait("tensor", ("psa", half))
                for c4 in range(4):
                    c = half * 4 + c4
                    p.op("tensor", "transpose", psa[:, half, c4 * 128:(c4 + 1) * 128],
                         xntokA[:, b, c * 128:(c + 1) * 128], ident_f[:])
            v_trf = pe_flush()
            p.wait("vector", "pe", v_trb)
            p.op("vector", "tensor_copy",
                 xn_fm[:, :, tt * 128:(tt + 1) * 128],
                 psx[:].rearrange("p (c n) -> p c n", c=DC), inc="dve")
            guards["psx"] = ("dve", p.val("dve"))
            guards[("xnbfA", b)] = ("pe", v_trb)
            p.wait("vector", "pe", v_trf)
            for half in range(2):
                p.op("vector", "tensor_copy",
                     result[:, half * 4:(half + 1) * 4, tt * 128:(tt + 1) * 128],
                     psa[:, half, :].rearrange("p (c n) -> p c n", c=4), inc="dve")
                guards[("psa", half)] = ("dve", p.val("dve"))
            guards[("xntokA", b)] = ("pe", v_trf)
        vA_dve = p.val("dve")

        barrier()
        p.wait("vector", "dgb", v_bias_dma)
        p.wait("scalar", "dgb", v_bias_dma)

        if "resA" in debug:
            p.op("gpsimd", "dma_start", out=dbg_d["res"], in_=result[:], inc="dg")

        dvgw = p.op("sync", "dma_start", out=gate_st[:],
                    in_=gw_d.rearrange("(c p) e -> p c e", p=128), inc="dc")
        gate_sb = wstage[0:8, 0, :]
        p.wait("tensor", "dc", dvgw)
        for tb in range(TBN):
            bank = tb % 2
            guard_wait("tensor", ("psa", bank))
            for c in range(DC):
                p.op("tensor", "matmul", psa[0:8, bank, :], gate_st[:, c, :],
                     result[:, c, tb * NT:(tb + 1) * NT],
                     start=(c == 0), stop=(c == DC - 1),
                     inc="pe" if c == DC - 1 else None)
            v = p.val("pe")
            p.wait("vector", "pe", v)
            guard_wait("vector", "wrowstage")
            p.op("vector", "tensor_scalar", gate_sb[:, tb * NT:(tb + 1) * NT],
                 psa[0:8, bank, :], gatebp[0:8, :], None, ALU.add, inc="dve")
            guards[("psa", bank)] = ("dve", p.val("dve"))
        v_gate = p.val("dve")
        if "router2" in debug:
            p.wait("gpsimd", "dve", v_gate)
            p.op("gpsimd", "dma_start", out=dbg_d["gbp"], in_=gatebp[0:8, :], inc="dg")
            p.op("gpsimd", "dma_start", out=dbg_d["gsb"], in_=gate_sb, inc="dg")
        p.wait("tensor", "dve", v_gate)
        for tt in range(NTT):
            bank = 2 + (tt % 2)
            guard_wait("tensor", ("psa", bank))
            p.op("tensor", "transpose", psa[0:128, bank, 0:8],
                 gate_sb[:, tt * 128:(tt + 1) * 128], ident_f[0:8, 0:8])
            v = pe_flush()
            p.wait("vector", "pe", v)
            p.op("vector", "tensor_copy", rtok[:, tt * 8:(tt + 1) * 8], psa[:, bank, 0:8], inc="dve")
            guards[("psa", bank)] = ("dve", p.val("dve"))
        for tt in range(NTT):
            r8 = rtok[:, tt * 8:(tt + 1) * 8]
            w8 = wtok[:, tt * 8:(tt + 1) * 8]
            c = rtk
            def vf():
                p.wait("vector", "dve", p.val("dve"))
            p.op("vector", "reduce_max", c[:, 0:1], r8, axis=AX, negate=True, inc="dve")
            p.wait("scalar", "dve", p.val("dve"))
            p.op("scalar", "activation", w8, r8, AF.Exp, bias=c[:, 0:1], inc="act")
            p.wait("vector", "act", p.val("act"))
            p.op("vector", "reduce_sum", c[:, 1:2], w8, axis=AX, inc="dve")
            vf()
            p.op("vector", "reciprocal", c[:, 2:3], c[:, 1:2], inc="dve")
            vf()
            p.op("vector", "tensor_scalar", r8, w8, c[:, 2:3], None, ALU.mult, inc="dve")
            vf()
            p.op("vector", "reduce_max", c[:, 3:4], r8, axis=AX, inc="dve")
            vf()
            p.op("vector", "tensor_scalar", c[:, 8:16], r8, c[:, 3:4], None, ALU.is_equal, inc="dve")
            vf()
            p.op("vector", "scalar_tensor_tensor", w8, c[:, 8:16], -2.0, r8, ALU.mult, ALU.add, inc="dve")
            vf()
            p.op("vector", "reduce_max", c[:, 4:5], w8, axis=AX, inc="dve")
            vf()
            p.op("vector", "tensor_scalar", w8, r8, c[:, 4:5], None, ALU.is_ge, inc="dve")
            p.op("vector", "tensor_add", c[:, 5:6], c[:, 3:4], c[:, 4:5], inc="dve")
            vf()
            p.op("vector", "tensor_scalar", c[:, 5:6], c[:, 5:6], EPS_RT, None, ALU.add, inc="dve")
            vf()
            p.op("vector", "reciprocal", c[:, 6:7], c[:, 5:6], inc="dve")
            p.op("vector", "tensor_mul", w8, w8, r8, inc="dve")
            vf()
            p.op("vector", "tensor_scalar", w8, w8, c[:, 6:7], None, ALU.mult, inc="dve")
        v_wtok = p.val("dve")
        if "router2" in debug:
            p.wait("gpsimd", "dve", v_wtok)
            p.op("gpsimd", "dma_start", out=dbg_d["rtok"], in_=rtok, inc="dg")
            p.op("gpsimd", "dma_start", out=dbg_d["wtok"], in_=wtok, inc="dg")
        p.wait("tensor", "dve", v_wtok)
        for tt in range(NTT):
            bank = 2 + (tt % 2)
            guard_wait("tensor", ("psa", bank))
            p.op("tensor", "transpose", psa[0:8, bank, 0:128],
                 wtok[:, tt * 8:(tt + 1) * 8], ident_f[:])
            v = pe_flush()
            p.wait("vector", "pe", v)
            p.op("vector", "tensor_copy", wet[:, tt * 128:(tt + 1) * 128],
                 psa[0:8, bank, 0:128], inc="dve")
            guards[("psa", bank)] = ("dve", p.val("dve"))
        v_wet = p.val("dve")
        if "router" in debug:
            p.wait("gpsimd", "dve", v_wet)
            p.op("gpsimd", "dma_start", out=dbg_d["wet"], in_=wet, inc="dg")

        if phases >= 2:
            barrier()

            wbc_pool_guard = {}

            def build_wbc(e):
                guard_wait("gpsimd", "wrow")
                dv = p.op("gpsimd", "dma_start", out=wrow, in_=wet[e:e + 1, :], inc="dg")
                p.wait("tensor", "dg", dv)
                pg = wbc_pool_guard.get(e % 2)
                if pg:
                    p.wait("vector", "pool", pg)
                for tb in range(TBN):
                    guard_wait("tensor", ("psa", 6))
                    p.op("tensor", "matmul", psa[:, 6, :], ones_row[:],
                         wrow[:, tb * NT:(tb + 1) * NT], start=True, stop=True, inc="pe")
                    v = p.val("pe")
                    guards["wrow"] = ("pe", v)
                    p.wait("vector", "pe", v)
                    guard_wait("vector", ("wbc", e % 2))
                    p.op("vector", "tensor_copy", wbc[:, e % 2, tb * NT:(tb + 1) * NT],
                         psa[:, 6, :], inc="dve")
                    guards[("psa", 6)] = ("dve", p.val("dve"))
                return p.val("dve")

            wstage_slot = [0]

            def dma_w1(u, ig):
                dvs = []
                for half in range(2):
                    slot = wstage_slot[0]; wstage_slot[0] ^= 1
                    guard_wait("sync", ("wstage", slot))
                    src = w1_d[u].rearrange("(c p) i -> p c i", p=128)[
                        :, :, ig * 512 + half * 256: ig * 512 + (half + 1) * 256]
                    dv = p.op("sync", "dma_start",
                              out=wstage[:, slot, :].rearrange("p (c i) -> p c i", c=DC),
                              in_=src, inc=f"ds{slot}")
                    dvs.append((slot, dv))
                return dvs

            def conv_w1(ig, dvs):
                buf = ig % 2
                guard_wait("scalar", ("w1bf", buf))
                for half, (slot, dv) in enumerate(dvs):
                    p.wait("scalar", f"ds{slot}", dv)
                    p.op("scalar", "activation",
                         w1bf[:, buf, :, half * 256:(half + 1) * 256],
                         wstage[:, slot, :].rearrange("p (c i) -> p c i", c=DC), AF.Copy, inc="act")
                    guards[("wstage", slot)] = ("act", p.val("act"))
                return p.val("act")

            def dma_w2(u, ig):
                dvs = []
                for half in range(2):
                    slot = wstage_slot[0]; wstage_slot[0] ^= 1
                    guard_wait("sync", ("wstage", slot))
                    src = w2_d[u].rearrange("(g p) d -> p g d", p=128)[
                        :, ig * ICG + half * 2: ig * ICG + (half + 1) * 2, :]
                    dv = p.op("sync", "dma_start",
                              out=wstage[:, slot, :].rearrange("p (g d) -> p g d", g=2),
                              in_=src, inc=f"ds{slot}")
                    dvs.append((slot, dv))
                return dvs

            def conv_w2(ig, dvs):
                buf = ig % 2
                guard_wait("scalar", ("w2bf", buf))
                for half, (slot, dv) in enumerate(dvs):
                    p.wait("scalar", f"ds{slot}", dv)
                    p.op("scalar", "activation",
                         w2bf[:, buf, half * 2:(half + 1) * 2, :],
                         wstage[:, slot, :].rearrange("p (j d) -> p j d", j=2), AF.Copy, inc="act")
                    guards[("wstage", slot)] = ("act", p.val("act"))
                return p.val("act")

            wbc_ready = {0: build_wbc(0)}
            d1 = dma_w1(0, 0)
            w1_ready = conv_w1(0, d1)
            d2 = dma_w2(0, 0)

            gelu_j = [0]
            units = [(u, ig) for u in range(NU) for ig in range(NIG)]
            for idx, (u, ig) in enumerate(units):
                is_exp = u >= 1
                e = u - 1
                buf = ig % 2
                if is_exp and ig == 0:
                    p.wait("gpsimd", "dve", wbc_ready[e])
                w2_ready = conv_w2(ig, d2)
                nxt = units[idx + 1] if idx + 1 < len(units) else None
                if nxt:
                    d1 = dma_w1(*nxt)
                p.wait("tensor", "act", w1_ready)
                h_done = {}
                for ic in range(ICG):
                    icg = ig * ICG + ic
                    for tb in range(TBN):
                        bank = (ic * TBN + tb) % 3
                        guard_wait("tensor", ("psa", bank))
                        for c in range(DC):
                            p.op("tensor", "matmul", psa[:, bank, :],
                                 w1bf[:, buf, c, ic * 128:(ic + 1) * 128],
                                 xn_fm[:, c, tb * NT:(tb + 1) * NT],
                                 start=(c == 0), stop=(c == DC - 1),
                                 inc="pe" if c == DC - 1 else None)
                        v_h = p.val("pe")
                        guards[("w1bf", buf)] = ("pe", v_h)
                        p.wait("scalar", "pe", v_h)
                        bias_ap = ebias1[:, u * 32 + icg:u * 32 + icg + 1]
                        if is_exp:
                            j = gelu_j[0] % 2
                            gelu_j[0] += 1
                            guard_wait("scalar", ("g1", j))
                            p.op("scalar", "activation", g1[:, j, :], psa[:, bank, :],
                                 AF.Gelu_apprx_tanh, bias=bias_ap, inc="act")
                            v_gl = p.val("act")
                            guards[("psa", bank)] = ("act", v_gl)
                            p.wait("gpsimd", "act", v_gl)
                            guard_wait("gpsimd", "h_sb")
                            p.op("gpsimd", "tensor_mul", h_sb[:, ic, tb * NT:(tb + 1) * NT],
                                 g1[:, j, :], wbc[:, e % 2, tb * NT:(tb + 1) * NT], inc="pool")
                            guards[("g1", j)] = ("pool", p.val("pool"))
                            h_done[(ic, tb)] = ("pool", p.val("pool"))
                        else:
                            guard_wait("scalar", "h_sb")
                            p.op("scalar", "activation", h_sb[:, ic, tb * NT:(tb + 1) * NT],
                                 psa[:, bank, :], AF.Gelu_apprx_tanh, bias=bias_ap, inc="act")
                            guards[("psa", bank)] = ("act", p.val("act"))
                            h_done[(ic, tb)] = ("act", p.val("act"))
                if is_exp:
                    wbc_pool_guard[e % 2] = p.val("pool")
                if nxt:
                    w1_ready = conv_w1(nxt[1], d1)
                    d2 = dma_w2(*nxt)
                if ig == 0 and u + 1 < NU:
                    wbc_ready[u] = build_wbc(u)
                p.wait("tensor", "act", w2_ready)
                for dcc in range(DC):
                    for tb in range(TBN):
                        bank = 3 + (dcc * TBN + tb) % 3
                        guard_wait("tensor", ("psa", bank))
                        sem_h, v_hh = h_done[(ICG - 1, tb)]
                        p.wait("tensor", sem_h, v_hh)
                        for j in range(ICG):
                            p.op("tensor", "matmul", psa[:, bank, :],
                                 w2bf[:, buf, j, dcc * 128:(dcc + 1) * 128],
                                 h_sb[:, j, tb * NT:(tb + 1) * NT],
                                 start=(j == 0), stop=(j == ICG - 1),
                                 inc="pe" if j == ICG - 1 else None)
                        v_o = p.val("pe")
                        guards[("w2bf", buf)] = ("pe", v_o)
                        p.wait("vector", "pe", v_o)
                        p.op("vector", "tensor_tensor",
                             result[:, dcc, tb * NT:(tb + 1) * NT],
                             result[:, dcc, tb * NT:(tb + 1) * NT],
                             psa[:, bank, :], ALU.add, inc="dve")
                        guards[("psa", bank)] = ("dve", p.val("dve"))
                guards["h_sb"] = ("pe", p.val("pe"))
                if ig == NIG - 1:
                    if is_exp:
                        for dcc in range(DC):
                            p.op("vector", "scalar_tensor_tensor",
                                 result[:, dcc, :], wbc[:, e % 2, :],
                                 ebias2[:, u * 8 + dcc:u * 8 + dcc + 1],
                                 result[:, dcc, :], ALU.mult, ALU.add, inc="dve")
                        guards[("wbc", e % 2)] = ("dve", p.val("dve"))
                    else:
                        for dcc in range(DC):
                            p.op("vector", "tensor_scalar", result[:, dcc, :], result[:, dcc, :],
                                 ebias2[:, u * 8 + dcc:u * 8 + dcc + 1], None, ALU.add, inc="dve")
            v_ffn = p.val("dve")
            if "ffn" in debug:
                p.wait("gpsimd", "dve", v_ffn)
                p.op("gpsimd", "dma_start", out=dbg_d["res"], in_=result[:], inc="dg")

        if phases >= 3:
            barrier()

            for dcc in range(DC):
                p.op("scalar", "activation", result[:, dcc, :], result[:, dcc, :],
                     AF.Silu, inc="act")
            v_g = p.val("act")
            if "g" in debug:
                p.wait("gpsimd", "act", v_g)
                p.op("gpsimd", "dma_start", out=dbg_d["res"], in_=result[:], inc="dg")

            def stream_qkv(src_ap, slot):
                guard_wait("sync", ("wqkv_st", slot))
                dv = p.op("sync", "dma_start", out=wqkv_st[:, slot, :, :],
                          in_=src_ap.rearrange("(c p) k -> p c k", p=128), inc=f"ds{slot}")
                p.wait("scalar", f"ds{slot}", dv)
                guard_wait("scalar", ("wqkv_bf", slot))
                p.op("scalar", "activation",
                     wqkv_bf[:, slot, :, :].rearrange("p c k -> p (c k)"),
                     wqkv_st[:, slot, :, :].rearrange("p c k -> p (c k)"), AF.Copy, inc="act")
                guards[("wqkv_st", slot)] = ("act", p.val("act"))
                return p.val("act")

            def proj_2048(slot, dst, vready, banks=(0, 1)):
                p.wait("tensor", "act", vready)
                for nb in range(TBN):
                    bank = banks[nb % len(banks)]
                    guard_wait("tensor", ("psa", bank))
                    for c in range(DC):
                        p.op("tensor", "matmul", psa[:, bank, :],
                             wqkv_bf[:, slot, c, :],
                             xn_fm[:, c, nb * NT:(nb + 1) * NT],
                             start=(c == 0), stop=(c == DC - 1),
                             inc="pe" if c == DC - 1 else None)
                    v = p.val("pe")
                    guards[("wqkv_bf", slot)] = ("pe", v)
                    p.wait("vector", "pe", v)
                    p.op("vector", "tensor_copy", dst[:, nb * NT:(nb + 1) * NT],
                         psa[:, bank, :], inc="dve")
                    guards[("psa", bank)] = ("dve", p.val("dve"))
                return p.val("dve")

            v_ksb = proj_2048(0, k_sb, stream_qkv(wk_d, 0))
            v_vsb = proj_2048(1, v_sb, stream_qkv(wv_d, 1))
            p.wait("tensor", "dve", v_vsb)
            for rnd in range(2):
                guard_wait("tensor", "psx")
                for t8 in range(8):
                    mc = rnd * 8 + t8
                    p.op("tensor", "transpose", psx[:, t8 * 128:(t8 + 1) * 128],
                         v_sb[:, mc * 128:(mc + 1) * 128], ident_b[:])
                v = pe_flush()
                p.wait("vector", "pe", v)
                p.op("vector", "tensor_copy", vt_sb[:, rnd * 1024:(rnd + 1) * 1024],
                     psx[:], inc="dve")
                guards["psx"] = ("dve", p.val("dve"))
            v_vt_pe = p.val("pe")
            v_vt = p.val("dve")
            if "kv" in debug:
                p.wait("gpsimd", "dve", v_vt)
                p.op("gpsimd", "dma_start", out=dbg_d["kv"][:, 0, :], in_=k_sb, inc="dg")
                p.op("gpsimd", "dma_start", out=dbg_d["kv"][:, 1, :], in_=vt_sb, inc="dg")

            def proj_q(h):
                slot = h % 2
                vr = stream_qkv(wq_d[h], slot)
                p.wait("tensor", "act", vr)
                for nb in range(TBN):
                    guard_wait("tensor", ("psa", 5))
                    for c in range(DC):
                        p.op("tensor", "matmul", psa[:, 5, :],
                             wqkv_bf[:, slot, c, :],
                             xn_fm[:, c, nb * NT:(nb + 1) * NT],
                             start=(c == 0), stop=(c == DC - 1),
                             inc="pe" if c == DC - 1 else None)
                    v = p.val("pe")
                    guards[("wqkv_bf", slot)] = ("pe", v)
                    p.wait("vector", "pe", v)
                    guard_wait("vector", ("q_sb", h % 2))
                    p.op("vector", "tensor_copy", q_sb[:, h % 2, nb * NT:(nb + 1) * NT],
                         psa[:, 5, :], inc="dve")
                    guards[("psa", 5)] = ("dve", p.val("dve"))
                return p.val("dve")

            guards[("q_sb", 0)] = ("pe", v_vt_pe)
            guards[("psa", "lt", 0)] = guards.get(("psa", 0))
            guards[("psa", "lt", 1)] = guards.get(("psa", 1))
            q_ready = proj_q(0)
            guards[("psa", "lt", 2)] = guards.get(("psa", 5))

            pending = [None]

            def emit_finisher(fin):
                nb, otbank, v_recip = fin
                p.wait("tensor", "dve", v_recip)
                guard_wait("tensor", ("psa", 6))
                p.op("tensor", "matmul", psa[:, 6, :], ones_row[:], sumrec,
                     start=True, stop=True, inc="pe")
                v_rb = p.val("pe")
                guards["sumrec"] = ("pe", v_rb)
                p.wait("vector", "pe", v_rb)
                p.op("vector", "tensor_copy", rb_sb[:, nb % 2, :], psa[:, 6, :], inc="dve")
                guards[("psa", 6)] = ("dve", p.val("dve"))
                p.wait("vector", "dve", p.val("dve"))
                h_, ocat_ap = fin_ctx[nb]
                p.op("vector", "tensor_tensor", ocat_ap,
                     psa[:, otbank, :], rb_sb[:, nb % 2, :], ALU.mult, inc="dve")
                guards[("psa", otbank)] = ("dve", p.val("dve"))

            fin_ctx = {}
            for h in range(NH):
                p.wait("tensor", "dve", q_ready)
                for nb in range(TBN):
                    otbank = 2 + nb % 2
                    guard_wait("tensor", ("psa", otbank), ("psa", 4))

                    LTB = (0, 1, 5)

                    def emit_L(mc):
                        ltbank = LTB[mc % 3]
                        guard_wait("tensor", ("psa", "lt", mc % 3))
                        p.op("tensor", "matmul", psa[:, ltbank, :],
                             k_sb[:, mc * 128:(mc + 1) * 128],
                             q_sb[:, h % 2, nb * NT:(nb + 1) * NT],
                             start=True, stop=True, inc="pe")
                        return p.val("pe")

                    v_l = {0: emit_L(0), 1: emit_L(1)}
                    for mc in range(16):
                        ltbank = LTB[mc % 3]
                        if mc + 2 < 16:
                            v_l[mc + 2] = emit_L(mc + 2)
                        if mc == 1 and pending[0] is not None:
                            emit_finisher(pending[0])
                            pending[0] = None
                        p.wait("scalar", "pe", v_l[mc])
                        guard_wait("scalar", ("e_sb", mc % 2))
                        p.op("scalar", "activation", e_sb[:, mc % 2, :], psa[:, ltbank, :],
                             AF.Exp, inc="act")
                        v_e = p.val("act")
                        guards[("psa", "lt", mc % 3)] = ("act", v_e)
                        p.wait("tensor", "act", v_e)
                        p.op("tensor", "matmul", psa[:, otbank, :],
                             vt_sb[:, mc * 128:(mc + 1) * 128], e_sb[:, mc % 2, :],
                             start=(mc == 0), stop=(mc == 15))
                        p.op("tensor", "matmul", psa[0:1, 4, :],
                             ones_col[:], e_sb[:, mc % 2, :],
                             start=(mc == 0), stop=(mc == 15), inc="pe")
                        guards[("e_sb", mc % 2)] = ("pe", p.val("pe"))
                    v_osum = p.val("pe")
                    p.wait("vector", "pe", v_osum)
                    guard_wait("vector", "sumrec")
                    p.op("vector", "reciprocal", sumrec, psa[0:1, 4, :], inc="dve")
                    v_recip = p.val("dve")
                    guards[("psa", 4)] = ("dve", v_recip)
                    fin_ctx[nb] = (h, ocat[:, h, nb * NT:(nb + 1) * NT])
                    pending[0] = (nb, otbank, v_recip)
                guards[("q_sb", h % 2)] = ("pe", p.val("pe"))
                if h + 1 < NH:
                    guards[("psa", 5)] = guards.get(("psa", "lt", 2))
                    q_ready = proj_q(h + 1)
                    guards[("psa", "lt", 2)] = guards.get(("psa", 5))
            emit_finisher(pending[0])
            pending[0] = None
            v_attn_pe = p.val("pe")
            v_ocat = p.val("dve")
            if "ocat" in debug:
                p.wait("gpsimd", "dve", v_ocat)
                p.op("gpsimd", "dma_start", out=dbg_d["ocat"],
                     in_=ocat.rearrange("p h n -> p (h n)"), inc="dg")

            wo_slot = [0]
            for hh in range(NH):
                slot = wo_slot[0]; wo_slot[0] ^= 1
                guard_wait("sync", ("wo_st", slot))
                dv = p.op("sync", "dma_start", out=wo_st[:, slot, :], in_=wot_d[hh], inc=f"dc{slot}")
                p.wait("scalar", f"dc{slot}", dv)
                if hh == 0:
                    p.wait("scalar", "pe", v_attn_pe)
                p.op("scalar", "activation", wo_bf[:, hh, :], wo_st[:, slot, :], AF.Copy, inc="act")
                guards[("wo_st", slot)] = ("act", p.val("act"))
            v_wo = p.val("act")

            out_rows = out_d.rearrange("(t p) d -> t p d", p=128)
            xtok_dma = {}
            for ns in range(2):
                xtok_dma[ns] = p.op("sync", "dma_start", out=xtokC[:, ns % 2, :],
                                    in_=x_rows[ns], inc=f"ds{ns % 2}")
            p.wait("tensor", "act", v_wo)
            p.wait("tensor", "dve", v_ocat)
            guards[("psa", 0)] = guards.get(("psa", "lt", 0))
            guards[("psa", 1)] = guards.get(("psa", "lt", 1))
            for ns in range(NTT):
                yb = 2 * (ns % 2)
                gbank = 4 + ns % 2
                guard_wait("tensor", ("psa", yb), ("psa", yb + 1))
                for hh in range(NH):
                    for half in range(2):
                        p.op("tensor", "matmul", psa[:, yb + half, :],
                             ocat[:, hh, ns * 128:(ns + 1) * 128],
                             wo_bf[:, hh, half * 512:(half + 1) * 512],
                             start=(hh == 0), stop=(hh == NH - 1),
                             inc="pe" if (hh == NH - 1 and half == 1) else None)
                v_y = p.val("pe")
                for half in range(2):
                    guard_wait("tensor", ("psa", gbank))
                    for c4 in range(4):
                        c = half * 4 + c4
                        p.op("tensor", "transpose", psa[:, gbank, c4 * 128:(c4 + 1) * 128],
                             result[:, c, ns * 128:(ns + 1) * 128], ident_f[:])
                    v_t = pe_flush()
                    p.wait("vector", "pe", v_t)
                    guard_wait("vector", ("gt", ns % 2))
                    p.op("vector", "tensor_copy",
                         gt_sb[:, ns % 2, half * 512:(half + 1) * 512],
                         psa[:, gbank, :], inc="dve")
                    guards[("psa", gbank)] = ("dve", p.val("dve"))
                p.wait("vector", "pe", v_y)
                p.wait("vector", f"ds{ns % 2}", xtok_dma[ns])
                p.wait("vector", "dve", p.val("dve"))
                yview = psa[:, yb:yb + 2, :].rearrange("p a b -> p (a b)")
                p.op("vector", "tensor_tensor", gt_sb[:, ns % 2, :], yview,
                     gt_sb[:, ns % 2, :], ALU.mult, inc="dve")
                p.wait("vector", "dve", p.val("dve"))
                p.op("vector", "tensor_tensor", xtokC[:, ns % 2, :], xtokC[:, ns % 2, :],
                     gt_sb[:, ns % 2, :], ALU.add, inc="dve")
                v_ep = p.val("dve")
                guards[("psa", yb)] = ("dve", v_ep)
                guards[("psa", yb + 1)] = ("dve", v_ep)
                guards[("gt", ns % 2)] = ("dve", v_ep)
                p.wait("sync", "dve", v_ep)
                p.op("sync", "dma_start", out=out_rows[ns], in_=xtokC[:, ns % 2, :], inc="dc")
                guards[("xtokC", ns % 2)] = ("dc", p.val("dc"))
                if ns + 2 < NTT:
                    guard_wait("sync", ("xtokC", ns % 2))
                    xtok_dma[ns + 2] = p.op("sync", "dma_start", out=xtokC[:, ns % 2, :],
                                            in_=x_rows[ns + 2], inc=f"ds{ns % 2}")
            p.wait("scalar", "dc", p.val("dc"))
            if debug:
                p.wait("gpsimd", "dg", p.val("dg"))
        else:
            if debug:
                p.wait("gpsimd", "dg", p.val("dg"))
            p.wait("gpsimd", "dve", p.val("dve"))
            p.op("gpsimd", "dma_start",
                 out=out_d.rearrange("(c p) d -> p c d", p=128),
                 in_=result[:].rearrange("p c n -> p (c n)").rearrange("p (a b) -> p a b", a=16),
                 inc="dg")
            p.wait("gpsimd", "dg", p.val("dg"))

        p.check()

        with nc.Block() as block:
            @block.tensor
            def _(h):
                for f in p.ops["tensor"]:
                    f(h, sems)

            @block.vector
            def _(h):
                for f in p.ops["vector"]:
                    f(h, sems)

            @block.scalar
            def _(h):
                for f in p.ops["scalar"]:
                    f(h, sems)

            @block.gpsimd
            def _(h):
                for f in p.ops["gpsimd"]:
                    f(h, sems)

            @block.sync
            def _(h):
                for f in p.ops["sync"]:
                    f(h, sems)

    return nc


def make_in_maps(L0_attn_x, L0_attn_proj_q, L0_attn_proj_k, L0_attn_proj_v, L0_attn_proj_o,
                 L0_ffn_gate_w, L0_ffn_gate_b, L0_ffn_sh_w1, L0_ffn_sh_b1, L0_ffn_sh_w2,
                 L0_ffn_sh_b2, L0_ffn_ew1, L0_ffn_eb1, L0_ffn_ew2, L0_ffn_eb2):
    f32 = np.float32
    common = {
        "wq": np.ascontiguousarray(L0_attn_proj_q, f32),
        "wk": np.ascontiguousarray(L0_attn_proj_k, f32),
        "wv": np.ascontiguousarray(L0_attn_proj_v, f32),
        "wot": np.ascontiguousarray(np.asarray(L0_attn_proj_o, f32).transpose(0, 2, 1)),
        "gw": np.ascontiguousarray(L0_ffn_gate_w, f32),
        "gb": np.ascontiguousarray(L0_ffn_gate_b, f32),
        "shw1": np.ascontiguousarray(L0_ffn_sh_w1, f32),
        "shb1": np.ascontiguousarray(L0_ffn_sh_b1, f32),
        "shw2": np.ascontiguousarray(L0_ffn_sh_w2, f32),
        "shb2": np.ascontiguousarray(L0_ffn_sh_b2, f32),
        "ew1": np.ascontiguousarray(L0_ffn_ew1, f32),
        "eb1": np.ascontiguousarray(L0_ffn_eb1, f32),
        "ew2": np.ascontiguousarray(L0_ffn_ew2, f32),
        "eb2": np.ascontiguousarray(L0_ffn_eb2, f32),
    }
    xs = np.ascontiguousarray(L0_attn_x, f32)
    return [dict(common, x=xs[c]) for c in range(8)]


def kernel(**inputs):
    nc = build_nc()
    in_maps = make_in_maps(**inputs)
    res = run_bass_kernel_spmd(nc, in_maps, core_ids=list(range(8)))
    return np.stack([res.results[c]["out"] for c in range(8)], axis=0)
